# version 30
# baseline (speedup 1.0000x reference)
import numpy as np
import ml_dtypes
import concourse.bass as bass
import concourse.mybir as mybir
from concourse.bass_utils import run_bass_kernel_spmd
from concourse.bass_core_rust import IndirectOffsetOnAxis
from contextlib import ExitStack

F32 = mybir.dt.float32
BF16 = mybir.dt.bfloat16
AF = mybir.ActivationFunctionType
ALU = mybir.AluOpType
AX = mybir.AxisListType.X
BF = ml_dtypes.bfloat16

L = 2048
D = 1024
H, G, HD = 16, 4, 64
WIN = 256
NT = 16
NC = 8
NP = 8

_l = np.arange(64)
_b, _r = _l // 32, _l % 32
PERM64 = np.where(_r < 16, 2 * (16 * _b + _r), 2 * (16 * _b + (_r - 16)) + 1)
SGN64 = np.where(_r < 16, -1.0, 1.0).astype(np.float32)
SHUF_MASK = [(i + 16) % 32 for i in range(32)]

CF_W = 176
C_IOTAC, C_BQ, C_BK, C_BO = 0, 16, 24, 28
C_ONESF, C_EPS, C_LC, C_LCT = 36, 164, 165, 169
CB_W = 4736
B_IDEN, B_ONES, B_MASKL, B_MASKR, B_ISHIFT, B_RCOS, B_RSIN = 0, 128, 256, 384, 512, 640, 2688

GB_EQO, GB_GCM, GB_EGC, GB_VGA, GB_XPV, GB_TGL = 0, 2048, 4096, 6144, 6404, 6469
GB_W = 6600
GB_S0 = GB_EGC
GB_S1 = GB_EGC + 1024

ES_SW = 644
FFN_NT_ES = 10

def strip_slots(tj):
    s0 = max(0, 2 - tj)
    s1 = 5 - max(0, tj + 2 - (NT - 1))
    return s0, s1


def pv_terms(qc):
    out = []
    for tj in range(max(0, 4 * qc - 2), min(NT, 4 * qc + 6)):
        qtl = max(tj - 2, 4 * qc, 0)
        qth = min(tj + 2, 4 * qc + 3, NT - 1)
        if qtl > qth:
            continue
        out.append((tj, qtl - (tj - 2), qth - qtl + 1, qtl))
    return out


def _check_geometry():
    seen = set()
    for qc in range(4):
        for tj, s0, n, qt0 in pv_terms(qc):
            for i in range(n):
                qt = qt0 + i
                s = s0 + i
                assert qt == tj - 2 + s and 0 <= qt < NT and abs(qt - tj) <= 2
                assert (tj, qt) not in seen
                seen.add((tj, qt))
    want = {(tj, qt) for tj in range(NT) for qt in range(NT) if abs(qt - tj) <= 2}
    assert seen == want


_check_geometry()


class Sched:
    ENGS = ("sync", "tensor", "vector", "scalar", "gpsimd")
    NDMA = 16

    def __init__(self, nc, stack):
        self.nc = nc
        self.sem = {e: stack.enter_context(nc.semaphore(f"sem_{e}")) for e in self.ENGS}
        self.dsem = [stack.enter_context(nc.semaphore(f"dma_{i}"))
                     for i in range(self.NDMA)]
        self.cnt = {e: 0 for e in self.ENGS}
        self.ops = {e: [] for e in self.ENGS}
        self.wm = {e: {} for e in self.ENGS}
        self.last_w = {}
        self.readers = {}
        self.dma_idx = 0
        self.stack = stack
        self.swsems = []

    def op(self, eng, fn, reads=(), writes=(), dma=False, swdge=False):
        need = {}

        def add(tok, c):
            if tok == ("e", "tensor") and eng == "tensor" and not dma:
                return
            if need.get(tok, 0) < c:
                need[tok] = c

        for k in reads:
            w = self.last_w.get(k)
            if w:
                add(*w)
        for k in writes:
            w = self.last_w.get(k)
            if w:
                add(*w)
            for tok, c in self.readers.get(k, {}).items():
                add(tok, c)
        if swdge:
            sem = self.stack.enter_context(
                self.nc.semaphore(f"swdma_{len(self.swsems)}"))
            self.swsems.append(sem)
            token = ("w", len(self.swsems) - 1)
            after = 16
            sem_inc = (sem, 16)
        elif dma:
            k_idx = self.dma_idx
            self.dma_idx += 1
            slot, rnd = k_idx % self.NDMA, k_idx // self.NDMA
            if rnd > 0:
                add(("d", slot), 16 * rnd)
            token = ("d", slot)
            after = 16 * (rnd + 1)
            sem_inc = (self.dsem[slot], 16)
        else:
            self.cnt[eng] += 1
            token = ("e", eng)
            after = self.cnt[eng]
            sem_inc = (self.sem[eng], 1)
        waits = []
        for tok, c in sorted(need.items()):
            if self.wm[eng].get(tok, 0) < c:
                sm = (self.sem[tok[1]] if tok[0] == "e"
                      else self.swsems[tok[1]] if tok[0] == "w"
                      else self.dsem[tok[1]])
                waits.append((sm, c))
                self.wm[eng][tok] = c
        self.ops[eng].append((waits, fn, sem_inc))
        for k in reads:
            d = self.readers.setdefault(k, {})
            if d.get(token, 0) < after:
                d[token] = after
        for k in writes:
            self.last_w[k] = (token, after)
            self.readers[k] = {}

    def emit(self, block):
        sched = self

        def section(e):
            def _(eng):
                for waits, fn, (sem, inc) in sched.ops[e]:
                    for (sm, c) in waits:
                        eng.wait_ge(sm, c)
                    fn(eng).then_inc(sem, inc)
                if e == "sync":
                    for i in range(sched.NDMA):
                        if sched.dma_idx > i:
                            tot = 16 * ((sched.dma_idx - 1 - i) // sched.NDMA + 1)
                            eng.wait_ge(sched.dsem[i], tot)
                    for sm in sched.swsems:
                        eng.wait_ge(sm, 16)
            return _

        for e in self.ENGS:
            getattr(block, e)(section(e))


def seg_keys(name, lo, hi):
    return [(name, s) for s in range(lo // 512, (hi + 511) // 512)]


def psum_keys(nc, t):
    b0 = nc.lookup_mloc(t).bank
    fb = int(np.prod(t.shape[1:])) * (4 if t.dtype == F32 else 2)
    nb = (fb + 2047) // 2048
    return [("pb", b0 + i) for i in range(nb)]


def build_program(debug=False):
    nc = bass.Bass("TRN2", target_bir_lowering=False)
    stack = ExitStack()

    xfm = nc.dram_tensor("xfm", [D, L], F32, kind="ExternalInput")
    xstd = nc.dram_tensor("xstd", [L, D], F32, kind="ExternalInput")
    wq_d = nc.dram_tensor("wq", [D, D], F32, kind="ExternalInput")
    wk_d = nc.dram_tensor("wk", [D, 512], F32, kind="ExternalInput")
    wv_d = nc.dram_tensor("wv", [D, 256], F32, kind="ExternalInput")
    wo_d = nc.dram_tensor("wo", [D, D], F32, kind="ExternalInput")
    gw_d = nc.dram_tensor("gwf", [128, 64], F32, kind="ExternalInput")
    ew_d = nc.dram_tensor("ewt", [8, D, D], F32, kind="ExternalInput")
    cf_d = nc.dram_tensor("cf", [128, CF_W], F32, kind="ExternalInput")
    cb_d = nc.dram_tensor("cb", [128, CB_W], BF16, kind="ExternalInput")
    tt_d = nc.dram_tensor("ttok", [128, 2048], BF16, kind="ExternalInput")
    br_d = nc.dram_tensor("brows", [1, 2048], F32, kind="ExternalInput")
    gpr_d = nc.dram_tensor("gpr", [128, 4], F32, kind="ExternalInput")
    gpc_d = nc.dram_tensor("gpc", [4, 1], F32, kind="ExternalInput")
    gpi_d = nc.dram_tensor("gpi", [4, 1], mybir.dt.int32, kind="ExternalInput")
    out_d = nc.dram_tensor("out", [L, D], F32, kind="ExternalOutput")
    dbg = {}
    if debug:
        for name, shp, dt in [
            ("dbg_kb", [128, 8192], BF16), ("dbg_qp", [128, 4096], BF16),
            ("dbg_es", [128, 10304], BF16), ("dbg_at", [128, 16384], BF16),
            ("dbg_vb", [128, 4160], BF16), ("dbg_xb", [128, 16384], BF16),
            ("dbg_ms", [128, 384], F32), ("dbg_g4b", [4, GB_W], BF16),
            ("dbg_em", [128, 68], BF16), ("dbg_eqt", [128, 64], BF16),
            ("dbg_xgf", [128, 32], BF16), ("dbg_kgf", [128, 16], BF16),
        ]:
            dbg[name] = nc.dram_tensor(name, shp, dt, kind="ExternalOutput")

    CF = nc.alloc_sbuf_tensor("CF", [128, CF_W], F32)
    CB = nc.alloc_sbuf_tensor("CB", [128, CB_W], BF16)
    XB = nc.alloc_sbuf_tensor("XB", [128, 16384], BF16)
    KB = nc.alloc_sbuf_tensor("KB", [128, 8192], BF16)
    VB = nc.alloc_sbuf_tensor("VB", [128, 4160], BF16)
    AT = nc.alloc_sbuf_tensor("AT", [128, 16384], BF16)
    QP = nc.alloc_sbuf_tensor("QP", [128, 4096], BF16)
    ES = nc.alloc_sbuf_tensor("ES", [128, 10304], BF16)
    WQ = nc.alloc_sbuf_tensor("WQ", [128, 8192], BF16)
    WK = nc.alloc_sbuf_tensor("WK", [128, 4096], BF16)
    WV = nc.alloc_sbuf_tensor("WV", [128, 2048], BF16)
    GWS = nc.alloc_sbuf_tensor("GWS", [128, 64], BF16)
    SC = nc.alloc_sbuf_tensor("SC", [128, 4096], F32)
    RS = nc.alloc_sbuf_tensor("RS", [128, 2560], BF16)
    ZR = nc.alloc_sbuf_tensor("ZR", [65, 2048], F32)
    MS = nc.alloc_sbuf_tensor("MS", [128, 384], F32)
    G4F = nc.alloc_sbuf_tensor("G4F", [4, 1024], F32)
    G4B = nc.alloc_sbuf_tensor("G4B", [4, GB_W], BF16)
    XGF = nc.alloc_sbuf_tensor("XGF", [128, 32], BF16)
    KGF = nc.alloc_sbuf_tensor("KGF", [128, 16], BF16)
    EM = nc.alloc_sbuf_tensor("EM", [128, 68], BF16)
    EQT = nc.alloc_sbuf_tensor("EQT", [128, 64], BF16)
    GPR = nc.alloc_sbuf_tensor("GPR", [128, 4], F32)
    GPC = nc.alloc_sbuf_tensor("GPC", [4, 1], F32)
    GPI = nc.alloc_sbuf_tensor("GPI", [4, 1], mybir.dt.int32)
    BRB = nc.alloc_sbuf_tensor("BRB", [1, 2048], BF16)

    sch = Sched(nc, stack)
    A = sch.op

    KCF, KCB = [("cf", 0)], [("cb", 0)]
    KBR = [("br", 0)]

    def dma(dst_ap, src_ap, reads, writes):
        A("sync", lambda e, d=dst_ap, s=src_ap: e.dma_start(d, s),
          reads=reads, writes=writes, dma=True)

    if debug:
        for t, nm, w in [(ES, "es", 10304), (QP, "qp", 4096), (RS, "rs", 2560),
                         (KB, "kb", 8192), (MS, "ms", 384), (G4B, "g4b", GB_W),
                         (VB, "vb", 4160), (AT, "at", 16384), (XB, "xb", 16384),
                         (EM, "em", 68), (EQT, "eqt", 64), (XGF, "xgf", 32),
                         (KGF, "kgf", 16), (WQ, "wq", 8192), (SC, "sc", 4096)]:
            A("vector", lambda e, t=t, w=w: e.memset(t[:, 0:w], 0.0),
              writes=seg_keys(nm, 0, w))
        A("vector", lambda e: e.memset(ZR[:], 0.0), writes=[("zr", 0), ("zr", 1)])
    dma(CF[:], cf_d[:], [], KCF)
    dma(CB[:], cb_d[:], [], KCB)
    dma(AT[:, 0:2048], tt_d[:], [], seg_keys("at", 0, 2048))
    dma(GPR[:], gpr_d[:], [], [("gpr", 0)])
    dma(GPC[:], gpc_d[:], [], [("gpc", 0)])
    dma(GPI[:], gpi_d[:], [], [("gpi", 0)])
    dma(SC[0:1, 0:2048], br_d[:], [], seg_keys("sc", 0, 2048))
    A("vector", lambda e: e.tensor_copy(BRB[:], SC[0:1, 0:2048]),
      reads=seg_keys("sc", 0, 2048), writes=[("brb", 0)])

    gstack = ExitStack()
    ip_tr = gstack.enter_context(nc.psum_tensor("ip_tr", [128, 512], BF16))
    ip_g1 = gstack.enter_context(nc.psum_tensor("ip_g1", [4, 512], F32))
    ip_g2 = gstack.enter_context(nc.psum_tensor("ip_g2", [4, 512], F32))
    ip_oc = gstack.enter_context(nc.psum_tensor("ip_oc", [128, 512], F32))
    KTR, KG1 = psum_keys(nc, ip_tr), psum_keys(nc, ip_g1)
    KG2, KOC = psum_keys(nc, ip_g2), psum_keys(nc, ip_oc)
    KMS = [("ms", 0)]
    KEM, KEQT, KG4B, KG4F = [("em", 0)], [("eqt", 0)], [("g4b", 0)], [("g4f", 0)]
    A("vector", lambda e: e.tensor_scalar(MS[0:4, 156:160], GPR[0:4, :], GPC[0:4, :],
                                          None, ALU.subtract),
      reads=[("gpr", 0), ("gpc", 0)], writes=KMS)
    A("vector", lambda e: e.tensor_scalar(MS[0:4, 160:164], MS[0:4, 156:160], 0.0,
                                          None, ALU.is_equal),
      reads=KMS, writes=KMS)
    A("vector", lambda e: e.tensor_tensor(MS[0:4, 164:168], MS[0:4, 160:164],
                                          CF[0:4, C_LC:C_LC + 4], ALU.mult),
      reads=KMS + KCF, writes=KMS)
    A("vector", lambda e: e.reduce_sum(MS[0:4, 168:169], MS[0:4, 164:168], axis=AX),
      reads=KMS, writes=KMS)
    A("vector", lambda e: e.tensor_scalar(MS[0:4, 169:170], MS[0:4, 168:169], 0.0,
                                          None, ALU.is_equal),
      reads=KMS, writes=KMS)
    A("vector", lambda e: e.tensor_tensor(MS[0:4, 170:174], MS[0:4, 160:164],
                                          CF[0:4, C_LCT:C_LCT + 4], ALU.mult),
      reads=KMS + KCF, writes=KMS)
    A("vector", lambda e: e.tensor_copy(RS[0:4, 0:4], MS[0:4, 170:174]),
      reads=KMS, writes=seg_keys("rs", 0, 4))
    A("tensor", lambda e: e.matmul(ip_g1[0:1, 0:4], CB[0:4, B_ONES:B_ONES + 1],
                                   RS[0:4, 0:4], start=True, stop=True),
      reads=seg_keys("rs", 0, 4) + KCB, writes=KG1)
    A("vector", lambda e: e.tensor_scalar(MS[0:1, 174:178], ip_g1[0:1, 0:4], 0.0,
                                          None, ALU.is_equal),
      reads=KG1, writes=KMS)
    A("vector", lambda e: e.tensor_copy(RS[0:1, 4:8], MS[0:1, 174:178]),
      reads=KMS, writes=seg_keys("rs", 0, 8))
    A("tensor", lambda e: e.matmul(ip_oc[:, 0:4], CB[0:1, B_ONES:B_ONES + 128],
                                   RS[0:1, 4:8], start=True, stop=True),
      reads=seg_keys("rs", 0, 8) + KCB, writes=KOC)
    A("vector", lambda e: e.tensor_copy(EM[:, 64:68], ip_oc[:, 0:4]),
      reads=KOC, writes=KEM)

    g0 = GPR[:, 0:4]
    gprb = bass.AP(g0.tensor, g0.offset, [g0.ap[0], [0, NT], [1, 4]])
    c0 = CF[:, C_IOTAC:C_IOTAC + NT]
    iotb = bass.AP(c0.tensor, c0.offset, [c0.ap[0], [1, NT], [0, 4]])
    A("vector", lambda e: e.tensor_tensor(
        MS[:, 176:240].rearrange("p (a b) -> p a b", b=4), gprb, iotb, ALU.subtract),
      reads=[("gpr", 0)] + KCF, writes=KMS)
    A("vector", lambda e: e.tensor_scalar(EQT[:, 0:64], MS[:, 176:240], 0.0,
                                          None, ALU.is_equal),
      reads=KMS, writes=KEQT)
    A("vector", lambda e: e.tensor_scalar(MS[:, 240:304], MS[:, 176:240],
                                          float(WIN), None, ALU.is_gt),
      reads=KMS, writes=KMS)
    A("vector", lambda e: e.tensor_scalar(MS[:, 304:368], MS[:, 176:240],
                                          float(-WIN), None, ALU.is_lt),
      reads=KMS, writes=KMS)
    A("vector", lambda e: e.tensor_tensor(MS[:, 240:304], MS[:, 240:304],
                                          MS[:, 304:368], ALU.add),
      reads=KMS, writes=KMS)
    e0 = EQT[:, 0:64]
    eqs = [bass.AP(e0.tensor, e0.offset + j, [e0.ap[0], [4, NT]]) for j in range(4)]
    A("vector", lambda e: e.tensor_tensor(MS[:, 368:384], eqs[0], eqs[1], ALU.add),
      reads=KEQT, writes=KMS)
    A("vector", lambda e: e.tensor_tensor(MS[:, 368:384], MS[:, 368:384], eqs[2],
                                          ALU.add),
      reads=KMS + KEQT, writes=KMS)
    A("vector", lambda e: e.tensor_tensor(MS[:, 368:384], MS[:, 368:384], eqs[3],
                                          ALU.add),
      reads=KMS + KEQT, writes=KMS)
    A("vector", lambda e: e.tensor_scalar(MS[:, 368:384], MS[:, 368:384], 1.0,
                                          None, ALU.min),
      reads=KMS, writes=KMS)
    A("vector", lambda e: e.tensor_scalar(MS[:, 368:384], MS[:, 368:384], -1.0, 1.0,
                                          ALU.mult, ALU.add),
      reads=KMS, writes=KMS)
    n0 = MS[:, 368:384]
    notb = bass.AP(n0.tensor, n0.offset, [n0.ap[0], [1, NT], [0, 4]])
    A("vector", lambda e: e.tensor_tensor(
        MS[:, 240:304].rearrange("p (a b) -> p a b", b=4),
        MS[:, 240:304].rearrange("p (a b) -> p a b", b=4), notb, ALU.mult),
      reads=KMS, writes=KMS)
    o0 = EM[:, 64:68]
    occb = bass.AP(o0.tensor, o0.offset, [o0.ap[0], [0, NT], [1, 4]])
    A("vector", lambda e: e.tensor_tensor(
        EM[:, 0:64].rearrange("p (a b) -> p a b", b=4),
        MS[:, 240:304].rearrange("p (a b) -> p a b", b=4), occb, ALU.mult),
      reads=KMS + KEM, writes=KEM)

    kzr_s = [("zr", 0), ("zr", 1)] + [("zrc", qc) for qc in range(4)]
    A("gpsimd", lambda e: e.iota(ZR[0:4, 0:2048], [[1, 2048]], base=0,
                                 channel_multiplier=0,
                                 allow_small_or_imprecise_dtypes=True),
      writes=kzr_s)
    A("vector", lambda e: e.tensor_scalar(ZR[0:4, 0:2048], ZR[0:4, 0:2048],
                                          GPC[0:4, :], None, ALU.subtract),
      reads=kzr_s + [("gpc", 0)], writes=kzr_s)
    A("vector", lambda e: e.tensor_scalar(G4B[0:4, GB_EQO:GB_EQO + 2048],
                                          ZR[0:4, 0:2048], 0.0, None, ALU.is_equal),
      reads=kzr_s, writes=KG4B)
    A("vector", lambda e: e.tensor_scalar(G4B[0:4, GB_EQO:GB_EQO + 2048],
                                          G4B[0:4, GB_EQO:GB_EQO + 2048],
                                          MS[0:4, 169:170], None, ALU.mult),
      reads=KG4B + KMS, writes=KG4B)
    A("vector", lambda e: e.tensor_scalar(G4B[0:4, GB_GCM:GB_GCM + 2048],
                                          ZR[0:4, 0:2048], float(WIN), None,
                                          ALU.is_gt),
      reads=kzr_s, writes=KG4B)
    A("vector", lambda e: e.tensor_scalar(ZR[0:4, 0:2048], ZR[0:4, 0:2048],
                                          float(-WIN), None, ALU.is_lt),
      reads=kzr_s, writes=kzr_s)
    A("vector", lambda e: e.tensor_tensor(G4B[0:4, GB_GCM:GB_GCM + 2048],
                                          G4B[0:4, GB_GCM:GB_GCM + 2048],
                                          ZR[0:4, 0:2048], ALU.add),
      reads=KG4B + kzr_s, writes=KG4B)
    A("vector", lambda e: e.tensor_scalar(G4B[0:4, GB_GCM:GB_GCM + 2048],
                                          G4B[0:4, GB_GCM:GB_GCM + 2048],
                                          MS[0:4, 169:170], None, ALU.mult),
      reads=KG4B + KMS, writes=KG4B)

    A("gpsimd", lambda e: e.indirect_dma_start(
        G4F[0:4, 0:1024], None, xstd[:, :],
        IndirectOffsetOnAxis(ap=GPI[:], axis=0)),
      reads=[("gpi", 0)], writes=KG4F, dma=True, swdge=True)
    A("vector", lambda e: e.tensor_copy(G4B[0:4, GB_S0:GB_S0 + 1024],
                                        G4F[0:4, 0:1024]),
      reads=KG4F, writes=KG4B)

    def transpose4(src_off, n_chunks, dst, dst_name, extra_reads):
        kst = seg_keys("at", 3072, 4096)
        A("vector", lambda e: e.memset(AT[:, 3072:4096], 0.0), writes=kst)
        A("vector", lambda e, so=src_off, n=n_chunks: e.tensor_copy(
            AT[0:4, 3072:3072 + 128 * n], G4B[0:4, so:so + 128 * n]),
          reads=KG4B + extra_reads, writes=kst)
        for cc in range(n_chunks):
            A("tensor", lambda e, cc=cc: e.transpose(
                ip_tr[:, 0:128], AT[:, 3072 + 128 * cc:3072 + 128 * (cc + 1)],
                CB[:, B_IDEN:B_IDEN + 128]),
              reads=kst + KCB, writes=KTR)
            A("vector", lambda e, cc=cc, dst=dst: e.tensor_copy(
                dst[:, 4 * cc:4 * (cc + 1)], ip_tr[:, 0:4]),
              reads=KTR, writes=[(dst_name, 0)])

    transpose4(GB_S0, 8, XGF, "xgf", [])

    for t in range(NT):
        A("tensor", lambda e, t=t: e.matmul(
            ip_g1[0:4, 0:128], EQT[:, 4 * t:4 * t + 4],
            AT[:, 128 * t:128 * (t + 1)], start=(t == 0), stop=(t == NT - 1)),
          reads=seg_keys("at", 128 * t, 128 * (t + 1)) + KEQT, writes=KG1)
    A("vector", lambda e: e.tensor_copy(G4B[0:4, GB_TGL:GB_TGL + 128],
                                        ip_g1[0:4, 0:128]),
      reads=KG1, writes=KG4B)

    wslot = [0]

    def load_weight(dram_ap_fn, nchunk, dst, dst_name, dst_off, width, eng="gpsimd"):
        for ci in range(nchunk):
            sl = (wslot[0] % 4) * 1024
            wslot[0] += 1
            ksc = seg_keys("sc", sl, sl + width)
            dma(SC[:, sl:sl + width], dram_ap_fn(ci), [], ksc)
            kd = seg_keys(dst_name, dst_off + ci * width, dst_off + (ci + 1) * width)
            A(eng, lambda e, ci=ci, sl=sl, dst=dst, w=width, o=dst_off: e.tensor_copy(
                dst[:, o + ci * w:o + (ci + 1) * w], SC[:, sl:sl + w]),
              reads=ksc, writes=kd)

    load_weight(lambda ci: wk_d[128 * ci:128 * (ci + 1), :], 8, WK, "wk", 0, 512)
    load_weight(lambda ci: wv_d[128 * ci:128 * (ci + 1), :], 8, WV, "wv", 0, 256)

    for c in range(NC):
        sl = 1024 * (wslot[0] % 4)
        wslot[0] += 2
        sl = 2048 * ((c % 2))
        ksc = seg_keys("sc", sl, sl + 2048)
        dma(SC[:, sl:sl + 2048], xfm[128 * c:128 * (c + 1), :], [], ksc)
        A("vector", lambda e, c=c, sl=sl: e.tensor_copy(
            XB[:, 2048 * c:2048 * (c + 1)], SC[:, sl:sl + 2048]),
          reads=ksc, writes=seg_keys("xb", 2048 * c, 2048 * (c + 1)))

    load_weight(lambda ci: wq_d[128 * ci:128 * (ci + 1), :], 8, WQ, "wq", 0, 1024)
    load_weight(lambda ci: gw_d[:, :], 1, GWS, "gw", 0, 64)

    with nc.psum_tensor("ip_kv", [128, 512], F32) as ip_kv, \
         nc.psum_tensor("ip_kv2", [128, 512], F32) as ip_kv2:
        KKV = psum_keys(nc, ip_kv)
        KKV2 = psum_keys(nc, ip_kv2)

        def rope_evac(kpsum, psum_ap, bias_col, scale, dst_ap, dst_keys, cosc, sinc):
            kr = seg_keys("rs", 0, 2048)
            if scale is None:
                A("vector", lambda e: e.tensor_scalar(
                    RS[:, 0:512], psum_ap, bias_col, None, ALU.add),
                  reads=kpsum + KCF, writes=kr)
            else:
                A("vector", lambda e: e.tensor_scalar(
                    RS[:, 0:512], psum_ap, bias_col, scale, ALU.add, ALU.mult),
                  reads=kpsum + KCF, writes=kr)
            A("vector", lambda e: e.stream_shuffle(RS[:, 512:1024], RS[:, 0:512], SHUF_MASK),
              reads=kr, writes=kr)
            A("vector", lambda e, cosc=cosc: e.tensor_tensor(
                RS[:, 1024:1536], RS[:, 0:512], CB[:, cosc:cosc + 512], ALU.mult),
              reads=kr + KCB, writes=kr)
            A("vector", lambda e, sinc=sinc: e.tensor_tensor(
                RS[:, 1536:2048], RS[:, 512:1024], CB[:, sinc:sinc + 512], ALU.mult),
              reads=kr + KCB, writes=kr)
            A("vector", lambda e, dst_ap=dst_ap: e.tensor_tensor(
                dst_ap, RS[:, 1024:1536], RS[:, 1536:2048], ALU.add),
              reads=kr, writes=dst_keys)

        for kc in range(4):
            for qc in range(4):
                kv, kkv = (ip_kv, KKV) if (4 * kc + qc) % 2 == 0 else (ip_kv2, KKV2)
                for ci in range(NC):
                    A("tensor", lambda e, kc=kc, qc=qc, ci=ci, kv=kv: e.matmul(
                        kv[:, 0:512], WK[:, 512 * ci + 128 * kc:512 * ci + 128 * (kc + 1)],
                        XB[:, 2048 * ci + 512 * qc:2048 * ci + 512 * (qc + 1)],
                        start=(ci == 0), stop=(ci == NC - 1)),
                      reads=seg_keys("wk", 512 * ci, 512 * (ci + 1)) +
                            seg_keys("xb", 2048 * ci + 512 * qc, 2048 * ci + 512 * (qc + 1)),
                      writes=kkv)
                rope_evac(kkv, kv[:, 0:512], CF[:, C_BK + kc:C_BK + kc + 1], None,
                          KB[:, 2048 * kc + 512 * qc:2048 * kc + 512 * (qc + 1)],
                          seg_keys("kb", 2048 * kc + 512 * qc, 2048 * kc + 512 * (qc + 1)),
                          B_RCOS + 512 * qc, B_RSIN + 512 * qc)

        A("vector", lambda e: e.memset(VB[:], 1.0), writes=seg_keys("vb", 0, 4160))
        for t in range(NT):
            kv, kkv = (ip_kv, KKV) if t % 2 == 0 else (ip_kv2, KKV2)
            for ci in range(NC):
                A("tensor", lambda e, t=t, ci=ci, kv=kv: e.matmul(
                    kv[:, 0:256], XB[:, 2048 * ci + 128 * t:2048 * ci + 128 * (t + 1)],
                    WV[:, 256 * ci:256 * (ci + 1)], start=(ci == 0), stop=False),
                  reads=seg_keys("wv", 256 * ci, 256 * (ci + 1)) +
                        seg_keys("xb", 2048 * ci + 128 * t, 2048 * ci + 128 * (t + 1)),
                  writes=kkv)
            A("tensor", lambda e, kv=kv: e.matmul(
                kv[:, 0:256], CB[0:1, B_ONES:B_ONES + 128], BRB[0:1, 1536:1792],
                start=False, stop=True),
              reads=[("brb", 0)] + KCB, writes=kkv)
            for g in range(4):
                A("vector", lambda e, t=t, g=g, kv=kv: e.tensor_copy(
                    VB[:, 260 * t + 65 * g:260 * t + 65 * g + 64],
                    kv[:, 64 * g:64 * (g + 1)]),
                  reads=kkv, writes=seg_keys("vb", 260 * t, 260 * (t + 1)))

        def glob_proj(w_sb, w_name, w_cw, dout, bias_off, scale, dst_off):
            nhalf = (dout + 511) // 512
            for half in range(nhalf):
                w0 = 512 * half
                wd = min(512, dout - w0)
                ps = ip_g1 if half == 0 else ip_g2
                kps = KG1 if half == 0 else KG2
                for ci in range(NC):
                    A("tensor", lambda e, ci=ci, ps=ps, w0=w0, wd=wd, w_sb=w_sb: e.matmul(
                        ps[0:4, 0:wd], XGF[:, 4 * ci:4 * (ci + 1)],
                        w_sb[:, w_cw * ci + w0:w_cw * ci + w0 + wd],
                        start=(ci == 0), stop=False),
                      reads=[("xgf", 0)] + seg_keys(w_name, w_cw * ci, w_cw * (ci + 1)),
                      writes=kps)
                A("tensor", lambda e, ps=ps, w0=w0, wd=wd: e.matmul(
                    ps[0:4, 0:wd], CB[0:1, B_ONES:B_ONES + 4],
                    BRB[0:1, bias_off + w0:bias_off + w0 + wd], start=False, stop=True),
                  reads=[("brb", 0)] + KCB, writes=kps)
                A("vector", lambda e, ps=ps, w0=w0, wd=wd: e.tensor_scalar(
                    G4F[0:4, dst_off + w0:dst_off + w0 + wd], ps[0:4, 0:wd],
                    scale, None, ALU.mult),
                  reads=kps, writes=KG4F)

        def glob_rope(src_f32_off, dout, dst_bf_off):
            nu = dout // 64
            A("vector", lambda e: e.tensor_copy(
                G4B[0:4, dst_bf_off:dst_bf_off + dout].rearrange("p (u w) -> p u w", w=64),
                G4F[0:4, src_f32_off:src_f32_off + dout].rearrange("p (u w) -> p u w", w=64)),
              reads=KG4F, writes=KG4B)
            src = G4B[0:4, dst_bf_off:dst_bf_off + dout]
            sw_in = bass.AP(src.tensor, src.offset + 16,
                            [src.ap[0], [32, dout // 32], [-16, 2], [1, 16]])
            A("vector", lambda e: e.tensor_copy(
                G4B[0:4, GB_S1:GB_S1 + dout].rearrange("p (a b c) -> p a b c", b=2, c=16),
                sw_in),
              reads=KG4B, writes=KG4B)
            cosap = G4B[0:4, GB_TGL:GB_TGL + 64]
            sinap = G4B[0:4, GB_TGL + 64:GB_TGL + 128]
            cosb = bass.AP(cosap.tensor, cosap.offset, [cosap.ap[0], [0, nu], [1, 64]])
            sinb = bass.AP(sinap.tensor, sinap.offset, [sinap.ap[0], [0, nu], [1, 64]])
            v3 = G4B[0:4, dst_bf_off:dst_bf_off + dout].rearrange("p (u w) -> p u w", w=64)
            s3 = G4B[0:4, GB_S1:GB_S1 + dout].rearrange("p (u w) -> p u w", w=64)
            A("vector", lambda e: e.tensor_tensor(v3, v3, cosb, ALU.mult),
              reads=KG4B, writes=KG4B)
            A("vector", lambda e: e.tensor_tensor(s3, s3, sinb, ALU.mult),
              reads=KG4B, writes=KG4B)
            A("vector", lambda e: e.tensor_tensor(v3, v3, s3, ALU.add),
              reads=KG4B, writes=KG4B)

        glob_proj(WK, "wk", 512, 512, 1024, 1.0, 0)
        glob_rope(0, 512, GB_S0)
        transpose4(GB_S0, 4, KGF, "kgf", [])
        for ci in range(NC):
            A("tensor", lambda e, ci=ci: e.matmul(
                ip_g1[0:4, 0:256], XGF[:, 4 * ci:4 * (ci + 1)],
                WV[:, 256 * ci:256 * (ci + 1)], start=(ci == 0), stop=False),
              reads=[("xgf", 0)] + seg_keys("wv", 256 * ci, 256 * (ci + 1)), writes=KG1)
        A("tensor", lambda e: e.matmul(
            ip_g1[0:4, 0:256], CB[0:1, B_ONES:B_ONES + 4],
            BRB[0:1, 1536:1792], start=False, stop=True),
          reads=[("brb", 0)] + KCB, writes=KG1)
        A("vector", lambda e: e.memset(G4B[0:4, GB_VGA:GB_VGA + 260], 1.0),
          writes=KG4B)
        for g in range(4):
            A("vector", lambda e, g=g: e.tensor_copy(
                G4B[0:4, GB_VGA + 65 * g:GB_VGA + 65 * g + 64],
                ip_g1[0:4, 64 * g:64 * (g + 1)]),
              reads=KG1, writes=KG4B)

        glob_proj(WQ, "wq", 1024, 1024, 0, 0.125, 0)
        glob_rope(0, 1024, GB_S0)
        transpose4(GB_S0, 8, XGF, "xgf", [])

    gstack.close()

    with nc.psum_tensor("ap_nm", [65, 2048], F32) as ap_nm, \
         nc.psum_tensor("ap_s0", [128, 768], F32) as ap_s0, \
         nc.psum_tensor("ap_s1", [128, 768], F32) as ap_s1:
        KNM = psum_keys(nc, ap_nm)
        SB2 = [ap_s0, ap_s1]
        KSC2 = [psum_keys(nc, ap_s0), psum_keys(nc, ap_s1)]
        KNMq = [[KNM[qc]] for qc in range(4)]

        def strip_region(h, tj):
            if tj >= 8:
                return ES, 5152 + ES_SW * (tj - 8), "es"
            if h % 2 == 0:
                return ES, ES_SW * tj, "es"
            if tj < 6:
                return WK, ES_SW * tj, "wk"
            return WV, ES_SW * (tj - 6), "wv"

        def strip_runs(h, t0, t1):
            runs = []
            for tj in range(t0, t1 + 1):
                t_, b_, k_ = strip_region(h, tj)
                if runs and runs[-1][0] is t_ and \
                   runs[-1][1] + ES_SW * (tj - runs[-1][3][0]) == b_:
                    runs[-1][3].append(tj)
                else:
                    runs.append([t_, b_, k_, [tj]])
            return runs

        def emit_gc(h):
            g, hb = h // 4, h % 2
            base = 64 * hb
            slot = 2048 * ((h // 2) % 2)
            kqp = seg_keys("qp", slot, slot + 2048)
            for q2 in range(4):
                ps, kps = SB2[q2 % 2], KSC2[q2 % 2]
                A("tensor", lambda e, q2=q2, base=base, g=g, slot=slot, ps=ps: e.matmul(
                    ps[0:4, 0:512], KGF[base:base + 64, 4 * g:4 * (g + 1)],
                    QP[base:base + 64, slot + 512 * q2:slot + 512 * (q2 + 1)],
                    start=True, stop=True),
                  reads=[("kgf", 0)] + kqp, writes=kps)
                A("scalar", lambda e, q2=q2, ps=ps: e.activation(
                    G4B[0:4, GB_EGC + 512 * q2:GB_EGC + 512 * (q2 + 1)],
                    ps[0:4, 0:512], AF.Exp),
                  reads=kps, writes=KG4B)
                A("vector", lambda e, q2=q2: e.tensor_tensor(
                    G4B[0:4, GB_EGC + 512 * q2:GB_EGC + 512 * (q2 + 1)],
                    G4B[0:4, GB_EGC + 512 * q2:GB_EGC + 512 * (q2 + 1)],
                    G4B[0:4, GB_GCM + 512 * q2:GB_GCM + 512 * (q2 + 1)], ALU.mult),
                  reads=KG4B, writes=KG4B)

        def emit_strips(h, t0, t1):
            g, hb = h // 4, h % 2
            p = h // 2
            base = 64 * hb
            slot = 2048 * (p % 2)
            kqp = seg_keys("qp", slot, slot + 2048)
            for tj in range(t0, t1 + 1):
                s0, s1 = strip_slots(tj)
                ap_sc = ap_s0 if tj % 2 == 0 else ap_s1
                KSC = KSC2[tj % 2]
                est, esb, esk = strip_region(h, tj)
                kes = seg_keys(esk, esb, esb + ES_SW)
                kkb = seg_keys("kb", 2048 * g + 128 * tj, 2048 * g + 128 * (tj + 1))
                a_hi = min(s1, 4)
                A("tensor", lambda e, tj=tj, s0=s0, a_hi=a_hi, ap_sc=ap_sc,
                  base=base, g=g, slot=slot: e.matmul(
                    ap_sc[:, 128 * s0:128 * a_hi],
                    KB[base:base + 64, 2048 * g + 128 * tj:2048 * g + 128 * (tj + 1)],
                    QP[base:base + 64,
                       slot + 128 * (tj - 2 + s0):slot + 128 * (tj - 2 + a_hi)],
                    start=True, stop=True),
                  reads=kkb + kqp, writes=KSC)
                if s1 == 5:
                    A("tensor", lambda e, tj=tj, base=base, g=g, slot=slot,
                      ap_sc=ap_sc: e.matmul(
                        ap_sc[:, 512:640],
                        KB[base:base + 64, 2048 * g + 128 * tj:2048 * g + 128 * (tj + 1)],
                        QP[base:base + 64, slot + 128 * (tj + 2):slot + 128 * (tj + 3)],
                        start=True, stop=False),
                      reads=kkb + kqp, writes=KSC)
                A("tensor", lambda e, tj=tj, base=base, p=p, g=g, s1=s1,
                  ap_sc=ap_sc: e.matmul(
                    ap_sc[:, 640:644],
                    KB[base:base + 64, 2048 * g + 128 * tj:2048 * g + 128 * (tj + 1)],
                    XGF[base:base + 64, 4 * p:4 * (p + 1)],
                    start=(s1 != 5), stop=True),
                  reads=kkb + [("xgf", 0)], writes=KSC)
                lo = 128 * s0
                hi = 640 if s1 == 5 else 128 * a_hi
                kese = [("ese", esk, esb)]
                if hi == 640:
                    A("scalar", lambda e, lo=lo, ap_sc=ap_sc, est=est, esb=esb:
                      e.activation(est[:, esb + lo:esb + 644],
                                   ap_sc[:, lo:644], AF.Exp),
                      reads=KSC, writes=kes + kese)
                else:
                    A("scalar", lambda e, lo=lo, hi=hi, ap_sc=ap_sc, est=est, esb=esb:
                      e.activation(est[:, esb + lo:esb + hi],
                                   ap_sc[:, lo:hi], AF.Exp),
                      reads=KSC, writes=kes)
                    A("scalar", lambda e, ap_sc=ap_sc, est=est, esb=esb:
                      e.activation(est[:, esb + 640:esb + 644],
                                   ap_sc[:, 640:644], AF.Exp),
                      reads=KSC, writes=kese)

        def emit_masks(h, mt0, mt1):
            def masked(c0, c1, mask_off, t0, t1):
                t0, t1 = max(t0, mt0), min(t1, mt1)
                if t0 > t1:
                    return
                for est, b0, k_, tjs in strip_runs(h, t0, t1):
                    n = len(tjs)
                    kk = seg_keys(k_, b0, b0 + ES_SW * n)
                    v = est[:, b0 + c0:b0 + c0 + 1]
                    vap = bass.AP(v.tensor, v.offset,
                                  [v.ap[0], [ES_SW, n], [1, c1 - c0]])
                    if mask_off is not None:
                        m = CB[:, mask_off:mask_off + 128]
                        map_ = bass.AP(m.tensor, m.offset,
                                       [m.ap[0], [0, n], [1, 128]])
                    else:
                        m = EM[:, 4 * tjs[0]:4 * tjs[0] + 1]
                        map_ = bass.AP(m.tensor, m.offset, [m.ap[0], [4, n], [1, 4]])
                    A("gpsimd", lambda e, vap=vap, map_=map_: e.tensor_tensor(
                        vap, vap, map_, ALU.mult),
                      reads=kk + KCB + KEM, writes=kk)
            masked(0, 128, B_MASKL, 2, 15)
            masked(512, 640, B_MASKR, 0, 13)
            for est, b0, k_, tjs in strip_runs(h, 0, 15):
                n = len(tjs)
                kk = [("ese", k_, b0 + ES_SW * i) for i in range(n)]
                v = est[:, b0 + 640:b0 + 641]
                vap = bass.AP(v.tensor, v.offset, [v.ap[0], [ES_SW, n], [1, 4]])
                m = EM[:, 4 * tjs[0]:4 * tjs[0] + 1]
                map_ = bass.AP(m.tensor, m.offset, [m.ap[0], [4, n], [1, 4]])
                A("gpsimd", lambda e, vap=vap, map_=map_: e.tensor_tensor(
                    vap, vap, map_, ALU.mult),
                  reads=kk + KEM, writes=kk)

        def emit_extra_pv(h):
            g = h // 4
            for tj in range(NT):
                est, esb, esk = strip_region(h, tj)
                A("tensor", lambda e, tj=tj, g=g, est=est, esb=esb: e.matmul(
                    ap_s0[0:4, 0:65], est[:, esb + 640:esb + 644],
                    VB[:, 260 * tj + 65 * g:260 * tj + 65 * (g + 1)],
                    start=(tj == 0), stop=(tj == NT - 1)),
                  reads=[("ese", esk, esb)] +
                        seg_keys("vb", 260 * tj, 260 * (tj + 1)),
                  writes=KSC2[0])
            A("vector", lambda e: e.tensor_copy(
                G4B[0:4, GB_XPV:GB_XPV + 65], ap_s0[0:4, 0:65]),
              reads=KSC2[0], writes=KG4B)

        def emit_pv_half(h, qh):
            g = h // 4
            for qc in (2 * qh, 2 * qh + 1):
                off = 512 * qc
                A("tensor", lambda e, qc=qc, off=off, g=g: e.matmul(
                    ap_nm[:, off:off + 512],
                    G4B[0:4, GB_VGA + 65 * g:GB_VGA + 65 * (g + 1)],
                    G4B[0:4, GB_EGC + 512 * qc:GB_EGC + 512 * (qc + 1)],
                    start=True, stop=False),
                  reads=KG4B, writes=KNMq[qc])
                for tj, sl0, ntl, qt0 in pv_terms(qc):
                    est, esb, esk = strip_region(h, tj)
                    A("tensor", lambda e, sl0=sl0, ntl=ntl, qt0=qt0, tj=tj,
                      g=g, est=est, esb=esb: e.matmul(
                        ap_nm[:, 128 * qt0:128 * (qt0 + ntl)],
                        VB[:, 260 * tj + 65 * g:260 * tj + 65 * (g + 1)],
                        est[:, esb + 128 * sl0:esb + 128 * (sl0 + ntl)],
                        start=False, stop=False),
                      reads=seg_keys(esk, esb, esb + ES_SW) +
                            seg_keys("vb", 260 * tj, 260 * (tj + 1)),
                      writes=KNMq[qc])
                A("tensor", lambda e, qc=qc, off=off: e.matmul(
                    ap_nm[:, off:off + 512],
                    G4B[0:4, GB_XPV:GB_XPV + 65],
                    G4B[0:4, GB_EQO + 512 * qc:GB_EQO + 512 * (qc + 1)],
                    start=False, stop=True),
                  reads=KG4B, writes=KNMq[qc])

        def emit_lnexp(h, qh):
            kzr = [("zr", qh)]
            A("scalar", lambda e, qh=qh: e.activation(
                ZR[64:65, 1024 * qh:1024 * (qh + 1)],
                ap_nm[64:65, 1024 * qh:1024 * (qh + 1)], AF.Ln),
              reads=KNMq[2 * qh] + KNMq[2 * qh + 1], writes=kzr)
            for qc in (2 * qh, 2 * qh + 1):
                rzo = 512 * qc
                krz = [("rzb", qc)] + seg_keys("rs", rzo, rzo + 512)
                A("scalar", lambda e, qc=qc, rzo=rzo: e.activation(
                    RS[64:65, rzo:rzo + 512], ZR[64:65, 512 * qc:512 * (qc + 1)],
                    AF.Exp, scale=-1.0),
                  reads=kzr, writes=krz)

        def emit_evac_rep(h, qh):
            for qc in (2 * qh, 2 * qh + 1):
                kzc = [("zrc", qc)]
                rzo = 512 * qc
                krz = [("rzb", qc)] + seg_keys("rs", rzo, rzo + 512)
                ps, kps = SB2[qc % 2], KSC2[qc % 2]
                A("tensor", lambda e, qc=qc, ps=ps, rzo=rzo: e.matmul(
                    ps[0:64, 0:512], CB[64:65, B_ONES:B_ONES + 64],
                    RS[64:65, rzo:rzo + 512], start=True, stop=True),
                  reads=krz + KCB, writes=kps)
                A("vector", lambda e, qc=qc, ps=ps: e.tensor_copy(
                    ZR[0:64, 512 * qc:512 * (qc + 1)], ps[0:64, 0:512]),
                  reads=kps, writes=kzc)

        def emit_evac_div(h, qh):
            p, hb = h // 2, h % 2
            for qc in (2 * qh, 2 * qh + 1):
                off = 512 * qc
                kzc = [("zrc", qc)]
                ps2, kps2 = SB2[1 - qc % 2], KSC2[1 - qc % 2]
                if hb == 0:
                    A("vector", lambda e, p=p, qc=qc, off=off: e.tensor_tensor(
                        AT[0:64, 2048 * p + 512 * qc:2048 * p + 512 * (qc + 1)],
                        ap_nm[0:64, off:off + 512],
                        ZR[0:64, 512 * qc:512 * (qc + 1)], ALU.mult),
                      reads=KNMq[qc] + kzc,
                      writes=seg_keys("at", 2048 * p + 512 * qc,
                                      2048 * p + 512 * (qc + 1)))
                else:
                    krs = [("rsb", 0)]
                    A("vector", lambda e, qc=qc, off=off: e.tensor_tensor(
                        RS[0:64, 2048 + 256 * (qc % 2):2048 + 256 * (qc % 2) + 256]
                        if False else RS[0:64, 2048:2560],
                        ap_nm[0:64, off:off + 512],
                        ZR[0:64, 512 * qc:512 * (qc + 1)], ALU.mult),
                      reads=KNMq[qc] + kzc, writes=krs)
                    A("tensor", lambda e, ps2=ps2: e.matmul(
                        ps2[:, 0:512], CB[0:64, B_ISHIFT:B_ISHIFT + 128],
                        RS[0:64, 2048:2560], start=True, stop=True),
                      reads=krs + KCB, writes=kps2)
                    A("vector", lambda e, p=p, qc=qc, ps2=ps2: e.tensor_copy(
                        AT[64:128, 2048 * p + 512 * qc:2048 * p + 512 * (qc + 1)],
                        ps2[64:128, 0:512]),
                      reads=kps2,
                      writes=seg_keys("at", 2048 * p + 512 * qc,
                                      2048 * p + 512 * (qc + 1)))

        def emit_qproj(p):
            slot = 2048 * (p % 2)
            for qc in range(4):
                ps, kps = SB2[qc % 2], KSC2[qc % 2]
                for ci in range(NC):
                    A("tensor", lambda e, p=p, qc=qc, ci=ci, ps=ps: e.matmul(
                        ps[:, 0:512],
                        WQ[:, 1024 * ci + 128 * p:1024 * ci + 128 * (p + 1)],
                        XB[:, 2048 * ci + 512 * qc:2048 * ci + 512 * (qc + 1)],
                        start=(ci == 0), stop=(ci == NC - 1)),
                      reads=seg_keys("wq", 1024 * ci, 1024 * (ci + 1)) +
                            seg_keys("xb", 2048 * ci + 512 * qc,
                                     2048 * ci + 512 * (qc + 1)),
                      writes=kps)
                rope_evac(kps, ps[:, 0:512], CF[:, C_BQ + p:C_BQ + p + 1], 0.125,
                          QP[:, slot + 512 * qc:slot + 512 * (qc + 1)],
                          seg_keys("qp", slot + 512 * qc, slot + 512 * (qc + 1)),
                          B_RCOS + 512 * qc, B_RSIN + 512 * qc)

        emit_qproj(0)
        emit_qproj(1)
        for h in range(16):
            emit_strips(h, 0, 7)
            emit_masks(h, 0, 7)
            if h > 0:
                emit_pv_half(h - 1, 0)
                emit_lnexp(h - 1, 0)
                emit_pv_half(h - 1, 1)
                emit_lnexp(h - 1, 1)
            emit_gc(h)
            emit_strips(h, 8, 15)
            emit_masks(h, 8, 15)
            if h > 0:
                emit_evac_rep(h - 1, 0)
                emit_evac_rep(h - 1, 1)
                emit_evac_div(h - 1, 0)
                emit_evac_div(h - 1, 1)
            if h % 2 == 1 and h < 12:
                emit_qproj(h // 2 + 2)
            emit_extra_pv(h)
            if h == 13:
                load_weight(lambda ci: wo_d[128 * ci:128 * (ci + 1), :], 8, WQ,
                            "wq", 0, 1024)
        emit_pv_half(15, 0)
        emit_lnexp(15, 0)
        emit_pv_half(15, 1)
        emit_lnexp(15, 1)
        emit_evac_rep(15, 0)
        emit_evac_rep(15, 1)
        emit_evac_div(15, 0)
        emit_evac_div(15, 1)
        if debug:
            ese_all = ([("ese", "es", ES_SW * tj) for tj in range(16)] +
                       [("ese", "wk", ES_SW * tj) for tj in range(6)] +
                       [("ese", "wv", ES_SW * tj) for tj in range(2)])
            for nm, t, w in [("dbg_kb", KB, 8192), ("dbg_qp", QP, 4096),
                             ("dbg_es", ES, 10304), ("dbg_vb", VB, 4160),
                             ("dbg_g4b", G4B, GB_W), ("dbg_em", EM, 68),
                             ("dbg_eqt", EQT, 64), ("dbg_xgf", XGF, 32),
                             ("dbg_kgf", KGF, 16)]:
                extra = ese_all if nm == "dbg_es" else []
                dma(dbg[nm][:], t[:, 0:w], seg_keys(nm[4:], 0, w) + extra, [])

        for co in range(NC):
            sl = (co % 2) * 2048
            ksc = seg_keys("sc", sl, sl + 2048)
            dma(SC[:, sl:sl + 2048], xfm[128 * co:128 * (co + 1), :], [], ksc)
            for qc in range(4):
                ps, kps = SB2[(co * 4 + qc) % 2], KSC2[(co * 4 + qc) % 2]
                for ci in range(NC):
                    A("tensor", lambda e, co=co, qc=qc, ci=ci, ps=ps: e.matmul(
                        ps[:, 0:512],
                        WQ[:, 1024 * ci + 128 * co:1024 * ci + 128 * (co + 1)],
                        AT[:, 2048 * ci + 512 * qc:2048 * ci + 512 * (qc + 1)],
                        start=(ci == 0), stop=(ci == NC - 1)),
                      reads=seg_keys("wq", 1024 * ci, 1024 * (ci + 1)) +
                            seg_keys("at", 2048 * ci + 512 * qc,
                                     2048 * ci + 512 * (qc + 1)),
                      writes=kps)
                A("vector", lambda e, co=co, qc=qc, sl=sl, ps=ps: e.scalar_tensor_tensor(
                    XB[:, 2048 * co + 512 * qc:2048 * co + 512 * (qc + 1)],
                    ps[:, 0:512], CF[:, C_BO + co:C_BO + co + 1],
                    SC[:, sl + 512 * qc:sl + 512 * (qc + 1)], ALU.add, ALU.add),
                  reads=kps + KCF + ksc,
                  writes=seg_keys("xb", 2048 * co + 512 * qc, 2048 * co + 512 * (qc + 1)))

    if debug:
        dma(dbg["dbg_xb"][:], XB[:], seg_keys("xb", 0, 16384), [])

    with nc.psum_tensor("lp_sum", [1, 2048], F32) as lp_sum, \
         nc.psum_tensor("lp_sq", [1, 2048], F32) as lp_sq:
        KSUM, KSQ = psum_keys(nc, lp_sum), psum_keys(nc, lp_sq)
        for c in range(NC):
            sqo = 2048 * (c % 2)
            krs = seg_keys("qp", sqo, sqo + 2048)
            if c % 4 == 3:
                A("scalar", lambda e, c=c, sqo=sqo: e.activation(
                    QP[:, sqo:sqo + 2048], XB[:, 2048 * c:2048 * (c + 1)], AF.Square),
                  reads=seg_keys("xb", 2048 * c, 2048 * (c + 1)), writes=krs)
            else:
                eng = "gpsimd" if c % 4 == 1 else "vector"
                A(eng, lambda e, c=c, sqo=sqo: e.tensor_tensor(
                    QP[:, sqo:sqo + 2048], XB[:, 2048 * c:2048 * (c + 1)],
                    XB[:, 2048 * c:2048 * (c + 1)], ALU.mult),
                  reads=seg_keys("xb", 2048 * c, 2048 * (c + 1)), writes=krs)
            for qc in range(4):
                A("tensor", lambda e, c=c, qc=qc: e.matmul(
                    lp_sum[0:1, 512 * qc:512 * (qc + 1)], CB[:, B_ONES:B_ONES + 1],
                    XB[:, 2048 * c + 512 * qc:2048 * c + 512 * (qc + 1)],
                    start=(c == 0), stop=(c == NC - 1)),
                  reads=seg_keys("xb", 2048 * c + 512 * qc, 2048 * c + 512 * (qc + 1)) + KCB,
                  writes=KSUM)
                A("tensor", lambda e, c=c, qc=qc, sqo=sqo: e.matmul(
                    lp_sq[0:1, 512 * qc:512 * (qc + 1)], CB[:, B_ONES:B_ONES + 1],
                    QP[:, sqo + 512 * qc:sqo + 512 * (qc + 1)],
                    start=(c == 0), stop=(c == NC - 1)),
                  reads=krs + KCB, writes=KSQ)
        ksc2 = seg_keys("sc", 0, 4096)
        A("vector", lambda e: e.tensor_scalar(
            SC[0:1, 0:2048], lp_sum[0:1, :], 1.0 / D, None, ALU.mult),
          reads=KSUM, writes=ksc2)
        A("vector", lambda e: e.tensor_tensor(
            SC[0:1, 2048:4096], SC[0:1, 0:2048], SC[0:1, 0:2048], ALU.mult),
          reads=ksc2, writes=ksc2)
        A("vector", lambda e: e.scalar_tensor_tensor(
            SC[0:1, 2048:4096], lp_sq[0:1, :], 1.0 / D, SC[0:1, 2048:4096],
            ALU.mult, ALU.subtract),
          reads=KSQ + ksc2, writes=ksc2)
        A("scalar", lambda e: e.activation(
            SC[0:1, 2048:4096], SC[0:1, 2048:4096], AF.Ln,
            bias=CF[0:1, C_EPS:C_EPS + 1]),
          reads=ksc2 + KCF, writes=ksc2)
        A("scalar", lambda e: e.activation(
            SC[0:1, 2048:4096], SC[0:1, 2048:4096], AF.Exp, scale=-0.5),
          reads=ksc2, writes=ksc2)

    def ffn_ap(nt):
        if nt < FFN_NT_ES:
            return ES[:, 1024 * nt:1024 * (nt + 1)], seg_keys("es", 1024 * nt,
                                                              1024 * (nt + 1))
        m = nt - FFN_NT_ES
        return WQ[:, 1024 * m:1024 * (m + 1)], seg_keys("wq", 1024 * m, 1024 * (m + 1))

    def wping_ap(e_idx, ci):
        if e_idx % 2 == 0:
            return KB[:, 1024 * ci:1024 * (ci + 1)], seg_keys("kb", 1024 * ci,
                                                              1024 * (ci + 1))
        if ci < 4:
            return QP[:, 1024 * ci:1024 * (ci + 1)], seg_keys("qp", 1024 * ci,
                                                              1024 * (ci + 1))
        return VB[:, 1024 * (ci - 4):1024 * (ci - 3)], seg_keys("vb", 1024 * (ci - 4),
                                                                1024 * (ci - 3))

    with nc.psum_tensor("mp_a", [128, 1024], F32) as mp_a, \
         nc.psum_tensor("mp_b", [128, 1024], F32) as mp_b, \
         nc.psum_tensor("mp_g", [128, 32], F32) as mp_g, \
         nc.psum_tensor("mp_g2", [128, 32], F32) as mp_g2, \
         nc.psum_tensor("mp_t", [128, 128], BF16) as mp_t, \
         nc.psum_tensor("mp_t2", [128, 128], BF16) as mp_t2:
        KYA, KYB = psum_keys(nc, mp_a), psum_keys(nc, mp_b)
        KGT, KYT = psum_keys(nc, mp_g), psum_keys(nc, mp_t)
        KGT2 = psum_keys(nc, mp_g2)
        KYT2 = psum_keys(nc, mp_t2)

        for qc in range(4):
            A("tensor", lambda e, qc=qc: e.matmul(
                mp_a[:, 0:512], CF[0:1, C_ONESF:C_ONESF + 128],
                SC[0:1, 512 * qc:512 * (qc + 1)], start=True, stop=True),
              reads=ksc2 + KCF, writes=KYA)
            A("tensor", lambda e, qc=qc: e.matmul(
                mp_a[:, 512:1024], CF[0:1, C_ONESF:C_ONESF + 128],
                SC[0:1, 2048 + 512 * qc:2048 + 512 * (qc + 1)], start=True, stop=True),
              reads=ksc2 + KCF, writes=KYA)
            st_off = 0
            xv = XB[:, 512 * qc:512 * qc + 1]
            xv3 = bass.AP(xv.tensor, xv.offset, [xv.ap[0], [2048, 8], [1, 512]])
            mr = mp_a[:, 0:512]
            mr3 = bass.AP(mr.tensor, mr.offset, [mr.ap[0], [0, 8], [1, 512]])
            rr = mp_a[:, 512:1024]
            rr3 = bass.AP(rr.tensor, rr.offset, [rr.ap[0], [0, 8], [1, 512]])
            av = AT[:, 512 * qc:512 * qc + 1]
            av3 = bass.AP(av.tensor, av.offset, [av.ap[0], [2048, 8], [1, 512]])
            qp0 = QP[:, 0:1]
            qp3 = bass.AP(qp0.tensor, qp0.offset, [qp0.ap[0], [512, 8], [1, 512]])
            kall = seg_keys("xb", 0, 16384)
            kat = seg_keys("at", 0, 16384)
            kqp = seg_keys("qp", 0, 4096)
            A("vector", lambda e, xv3=xv3, mr3=mr3, qp3=qp3: e.tensor_tensor(
                qp3, xv3, mr3, ALU.subtract),
              reads=kall + KYA, writes=kqp)
            A("vector", lambda e, av3=av3, rr3=rr3, qp3=qp3: e.tensor_tensor(
                av3, qp3, rr3, ALU.mult),
              reads=kqp + KYA, writes=kat)
            gp, kgp = (mp_g, KGT) if qc % 2 == 0 else (mp_g2, KGT2)
            for i in range(4):
                nt = 4 * qc + i
                for ci in range(NC):
                    A("tensor", lambda e, nt=nt, ci=ci, i=i, gp=gp: e.matmul(
                        gp[:, 8 * i:8 * (i + 1)],
                        AT[:, 2048 * ci + 128 * nt:2048 * ci + 128 * (nt + 1)],
                        GWS[:, 8 * ci:8 * (ci + 1)], start=(ci == 0), stop=(ci == NC - 1)),
                      reads=seg_keys("at", 2048 * ci + 128 * nt,
                                     2048 * ci + 128 * (nt + 1)) + [("gw", 0)],
                      writes=kgp)

            def sv(t, off, on, os_, inn, ins=1):
                t0 = t[:, 0:1]
                return bass.AP(t0.tensor, t0.offset + off,
                               [t0.ap[0], [os_, on], [ins, inn]])
            va, vb = sv(MS, 272, 4, 8, 4), sv(MS, 276, 4, 8, 4)
            ta, tb = sv(MS, 136, 4, 4, 4), sv(MS, 152, 4, 2, 2)
            ta0, ta1 = sv(MS, 136, 4, 4, 2), sv(MS, 138, 4, 4, 2)
            tb0, tb1 = sv(MS, 152, 4, 2, 1), sv(MS, 153, 4, 2, 1)
            m1c, m2c = sv(MS, 128, 4, 1, 1), sv(MS, 132, 4, 1, 1)
            m1b, m2b = sv(MS, 128, 4, 1, 8, 0), sv(MS, 132, 4, 1, 8, 0)
            v8 = sv(MS, 272, 4, 8, 8)
            oh1 = sv(MS, 160, 4, 8, 8)
            l2v = sv(MS, 192, 4, 8, 8)
            oh2 = sv(MS, 224, 4, 8, 8)
            w1b, w2b = sv(MS, 264, 4, 1, 8, 0), sv(MS, 268, 4, 1, 8, 0)
            cfo = sv(MS, 32 * qc, 4, 8, 8)
            A("vector", lambda e, gp=gp: e.tensor_copy(
                MS[:, 272:304], gp[:, 0:32]), reads=kgp, writes=KMS)
            A("vector", lambda e, va=va, vb=vb, ta=ta: e.tensor_tensor(
                ta, va, vb, ALU.max), reads=KMS, writes=KMS)
            A("vector", lambda e, ta0=ta0, ta1=ta1, tb=tb: e.tensor_tensor(
                tb, ta0, ta1, ALU.max), reads=KMS, writes=KMS)
            A("vector", lambda e, tb0=tb0, tb1=tb1, m1c=m1c: e.tensor_tensor(
                m1c, tb0, tb1, ALU.max), reads=KMS, writes=KMS)
            A("vector", lambda e, v8=v8, m1b=m1b, oh1=oh1: e.tensor_tensor(
                oh1, v8, m1b, ALU.is_equal), reads=KMS, writes=KMS)
            A("vector", lambda e, v8=v8, oh1=oh1, l2v=l2v: e.scalar_tensor_tensor(
                l2v, oh1, -1e9, v8, ALU.mult, ALU.add),
              reads=KMS, writes=KMS)
            la, lb = sv(MS, 192, 4, 8, 4), sv(MS, 196, 4, 8, 4)
            A("vector", lambda e, la=la, lb=lb, ta=ta: e.tensor_tensor(
                ta, la, lb, ALU.max), reads=KMS, writes=KMS)
            A("vector", lambda e, ta0=ta0, ta1=ta1, tb=tb: e.tensor_tensor(
                tb, ta0, ta1, ALU.max), reads=KMS, writes=KMS)
            A("vector", lambda e, tb0=tb0, tb1=tb1, m2c=m2c: e.tensor_tensor(
                m2c, tb0, tb1, ALU.max), reads=KMS, writes=KMS)
            A("vector", lambda e: e.tensor_tensor(
                MS[:, 256:260], MS[:, 132:136], MS[:, 128:132], ALU.subtract),
              reads=KMS, writes=KMS)
            A("scalar", lambda e: e.activation(
                MS[:, 260:264], MS[:, 256:260], AF.Exp),
              reads=KMS, writes=KMS)
            A("vector", lambda e: e.tensor_scalar(
                MS[:, 260:264], MS[:, 260:264], 1.0, None, ALU.add),
              reads=KMS, writes=KMS)
            A("vector", lambda e: e.reciprocal(MS[:, 264:268], MS[:, 260:264]),
              reads=KMS, writes=KMS)
            A("vector", lambda e: e.tensor_scalar(
                MS[:, 268:272], MS[:, 264:268], -1.0, 1.0, ALU.mult, ALU.add),
              reads=KMS, writes=KMS)
            A("vector", lambda e, l2v=l2v, m2b=m2b, oh2=oh2: e.tensor_tensor(
                oh2, l2v, m2b, ALU.is_equal), reads=KMS, writes=KMS)
            A("vector", lambda e, oh1=oh1, w1b=w1b, cfo=cfo: e.tensor_tensor(
                cfo, oh1, w1b, ALU.mult), reads=KMS, writes=KMS)
            A("vector", lambda e, oh2=oh2, w2b=w2b: e.tensor_tensor(
                oh2, oh2, w2b, ALU.mult), reads=KMS, writes=KMS)
            A("vector", lambda e, qc=qc, oh2=oh2, cfo=cfo: e.tensor_tensor(
                cfo, cfo, oh2, ALU.add), reads=KMS, writes=KMS)

        if debug:
            dma(dbg["dbg_ms"][:], MS[:], KMS, [])

        for ei in range(8):
            for ci in range(NC):
                sl = (wslot[0] % 4) * 1024
                wslot[0] += 1
                ksc = seg_keys("sc", sl, sl + 1024)
                dma(SC[:, sl:sl + 1024], ew_d[ei, 128 * ci:128 * (ci + 1), :], [], ksc)
                wap, kw = wping_ap(ei, ci)
                A("gpsimd", lambda e, sl=sl, wap=wap: e.tensor_copy(wap, SC[:, sl:sl + 1024]),
                  reads=ksc, writes=kw)
            for nt in range(NT):
                ye, kye = (mp_a, KYA) if (ei * NT + nt) % 2 == 0 else (mp_b, KYB)
                for ci in range(NC):
                    wap, kw = wping_ap(ei, ci)
                    for half in range(2):
                        A("tensor", lambda e, nt=nt, ci=ci, half=half, ye=ye, wap=wap: e.matmul(
                            ye[:, 512 * half:512 * (half + 1)],
                            AT[:, 2048 * ci + 128 * nt:2048 * ci + 128 * (nt + 1)],
                            wap[:, 512 * half:512 * (half + 1)],
                            start=(ci == 0), stop=(ci == NC - 1)),
                          reads=seg_keys("at", 2048 * ci + 128 * nt,
                                         2048 * ci + 128 * (nt + 1)) + kw,
                          writes=kye)
                fap, kf = ffn_ap(nt)
                if ei == 0:
                    A("vector", lambda e, nt=nt, ei=ei, ye=ye, fap=fap: e.tensor_scalar(
                        fap, ye[:], MS[:, 8 * nt + ei:8 * nt + ei + 1], None, ALU.mult),
                      reads=kye + KMS, writes=kf)
                else:
                    A("vector", lambda e, nt=nt, ei=ei, ye=ye, fap=fap: e.scalar_tensor_tensor(
                        fap, ye[:], MS[:, 8 * nt + ei:8 * nt + ei + 1], fap,
                        ALU.mult, ALU.add),
                      reads=kye + KMS + kf, writes=kf)
                if ei == 7:
                    par = nt % 2
                    par4 = nt % 4
                    mt, kyt = (mp_t, KYT) if par == 0 else (mp_t2, KYT2)
                    krs = [("rsy", par)]
                    kms2 = [("ms2", nt)]
                    mb = 192 + 8 * nt
                    for c in range(NC):
                        A("tensor", lambda e, c=c, nt=nt, mt=mt: e.transpose(
                            mt[:], AT[:, 2048 * c + 128 * nt:2048 * c + 128 * (nt + 1)],
                            CB[:, B_IDEN:B_IDEN + 128]),
                          reads=seg_keys("at", 2048 * c + 128 * nt,
                                         2048 * c + 128 * (nt + 1)) + KCB,
                          writes=kyt)
                        A("scalar", lambda e, c=c, par=par, mt=mt: e.copy(
                            RS[:, 1024 * par + 128 * c:1024 * par + 128 * (c + 1)], mt[:]),
                          reads=kyt, writes=krs)
                    ksc3 = [("scr", par4)]
                    A("vector", lambda e, fap=fap, par=par, par4=par4, mb=mb:
                      e.scalar_tensor_tensor(
                        KB[:, 1024 * par4:1024 * (par4 + 1)],
                        RS[:, 1024 * par:1024 * (par + 1)], 1.0, fap,
                        ALU.mult, ALU.add, accum_out=MS[:, mb:mb + 1]),
                      reads=krs + kf, writes=ksc3 + kms2)
                    A("vector", lambda e, par=par, par4=par4, mb=mb:
                      e.scalar_tensor_tensor(
                        KB[:, 4096 + 1024 * par:4096 + 1024 * (par + 1)],
                        KB[:, 1024 * par4:1024 * (par4 + 1)], 1.0,
                        KB[:, 1024 * par4:1024 * (par4 + 1)],
                        ALU.mult, ALU.mult, accum_out=MS[:, mb + 1:mb + 2]),
                      reads=ksc3, writes=[("kbs", par)] + kms2)
                    A("vector", lambda e, mb=mb: e.tensor_scalar(
                        MS[:, mb + 2:mb + 3], MS[:, mb:mb + 1], 1.0 / D, None, ALU.mult),
                      reads=kms2, writes=kms2)
                    A("vector", lambda e, mb=mb: e.tensor_tensor(
                        MS[:, mb + 3:mb + 4], MS[:, mb + 2:mb + 3], MS[:, mb + 2:mb + 3],
                        ALU.mult),
                      reads=kms2, writes=kms2)
                    A("vector", lambda e, mb=mb: e.scalar_tensor_tensor(
                        MS[:, mb + 4:mb + 5], MS[:, mb + 1:mb + 2], 1.0 / D,
                        MS[:, mb + 3:mb + 4], ALU.mult, ALU.subtract),
                      reads=kms2, writes=kms2)
                    A("scalar", lambda e, mb=mb: e.activation(
                        MS[:, mb + 5:mb + 6], MS[:, mb + 4:mb + 5], AF.Ln,
                        bias=CF[:, C_EPS:C_EPS + 1]),
                      reads=kms2 + KCF, writes=kms2)
                    A("scalar", lambda e, mb=mb: e.activation(
                        MS[:, mb + 6:mb + 7], MS[:, mb + 5:mb + 6], AF.Exp, scale=-0.5),
                      reads=kms2, writes=kms2)
                    so = 1024 * par4
                    kso = [("sco", par4)]
                    A("vector", lambda e, so=so, mb=mb, par4=par4: e.tensor_scalar(
                        SC[:, so:so + 1024], KB[:, 1024 * par4:1024 * (par4 + 1)],
                        MS[:, mb + 2:mb + 3], MS[:, mb + 6:mb + 7],
                        ALU.subtract, ALU.mult),
                      reads=ksc3 + kms2, writes=kso)
                    dma(out_d[128 * nt:128 * (nt + 1), :], SC[:, so:so + 1024], kso, [])

    if debug:
        dma(dbg["dbg_at"][:], AT[:], seg_keys("at", 0, 16384), [])

    with nc.Block() as block:
        sch.emit(block)
    stack.close()
    return nc


def prepare_inputs(inputs):
    x = np.asarray(inputs["L0_attn_x"], dtype=np.float32)
    gpos = np.asarray(inputs["L0_attn_global_positions"]).astype(np.int64)
    Wq = np.asarray(inputs["L0_attn_Wq"], dtype=np.float32)
    bq = np.asarray(inputs["L0_attn_bq"], dtype=np.float32)
    Wk = np.asarray(inputs["L0_attn_Wk"], dtype=np.float32)
    bk = np.asarray(inputs["L0_attn_bk"], dtype=np.float32)
    Wv = np.asarray(inputs["L0_attn_Wv"], dtype=np.float32)
    bv = np.asarray(inputs["L0_attn_bv"], dtype=np.float32)
    Wo = np.asarray(inputs["L0_attn_Wo"], dtype=np.float32)
    bo = np.asarray(inputs["L0_attn_bo"], dtype=np.float32)
    gw = np.asarray(inputs["L0_ffn_gate_w"], dtype=np.float32)
    ew = np.asarray(inputs["L0_ffn_expert_w"], dtype=np.float32)

    Wq_p = np.ascontiguousarray(Wq.reshape(D, H, HD)[:, :, PERM64].reshape(D, D))
    bq_p = bq.reshape(H, HD)[:, PERM64].reshape(D)
    Wk_p = Wk.reshape(D, G, HD)[:, :, PERM64]
    Wk_pd = np.ascontiguousarray(
        np.concatenate([Wk_p, Wk_p], axis=2).reshape(D, 512))
    bk_p = bk.reshape(G, HD)[:, PERM64]
    bk_pd = np.concatenate([bk_p, bk_p], axis=1).reshape(512)
    gwf = np.ascontiguousarray(gw.T.reshape(8, 128, 8).transpose(1, 0, 2)
                               .reshape(128, 64))
    ewt = np.ascontiguousarray(ew.transpose(0, 2, 1))

    inv_freq = 1.0 / 10000.0 ** (np.arange(0, HD, 2, dtype=np.float64) / HD)
    t = np.arange(L, dtype=np.float64)
    emb = np.concatenate([t[:, None] * inv_freq[None, :]] * 2, axis=1)
    cos_pl = np.cos(emb[:, PERM64])
    sin_pl = np.sin(emb[:, PERM64]) * SGN64[None, :]
    rcos = np.tile(cos_pl.T, (2, 1)).astype(BF)
    rsin = np.tile(sin_pl.T, (2, 1)).astype(BF)

    cf = np.zeros((128, CF_W), dtype=np.float32)
    cf[:, C_IOTAC:C_IOTAC + NT] = (np.arange(NT)[None, :] * 128
                                   + np.arange(128)[:, None]).astype(np.float32)
    cf[:, C_BQ:C_BQ + 8] = bq_p.reshape(8, 128).T
    cf[:, C_BK:C_BK + 4] = bk_pd.reshape(4, 128).T
    cf[:, C_BO:C_BO + 8] = bo.reshape(8, 128).T
    cf[:, C_ONESF:C_ONESF + 128] = 1.0
    cf[:, C_EPS] = 1e-5
    r4 = np.arange(4)
    cf[0:4, C_LC:C_LC + 4] = (r4[None, :] < r4[:, None]).astype(np.float32)
    cf[0:4, C_LCT:C_LCT + 4] = (r4[:, None] < r4[None, :]).astype(np.float32)

    cb = np.zeros((128, CB_W), dtype=BF)
    a128 = np.arange(128)
    cb[:, B_IDEN:B_IDEN + 128] = np.eye(128, dtype=BF)
    cb[:, B_ONES:B_ONES + 128] = np.float32(1.0)
    cb[:, B_MASKL:B_MASKL + 128] = (a128[None, :] >= a128[:, None]).astype(BF)
    cb[:, B_MASKR:B_MASKR + 128] = (a128[None, :] <= a128[:, None]).astype(BF)
    ish = np.zeros((128, 128), dtype=np.float32)
    ish[np.arange(64), np.arange(64) + 64] = 1.0
    cb[:, B_ISHIFT:B_ISHIFT + 128] = ish.astype(BF)
    cb[:, B_RCOS:B_RCOS + L] = rcos
    cb[:, B_RSIN:B_RSIN + L] = rsin

    tt = np.zeros((128, 2048), dtype=BF)
    for tt_i in range(NT):
        rows = slice(128 * tt_i, 128 * (tt_i + 1))
        tt[:, 128 * tt_i:128 * tt_i + 64] = cos_pl[rows].astype(BF)
        tt[:, 128 * tt_i + 64:128 * tt_i + 128] = sin_pl[rows].astype(BF)

    brows = np.zeros((1, 2048), dtype=np.float32)
    brows[0, 0:1024] = bq_p
    brows[0, 1024:1536] = bk_pd
    brows[0, 1536:1792] = bv

    gpr = np.tile(gpos.astype(np.float32)[None, :], (128, 1))
    gpc = gpos.astype(np.float32).reshape(4, 1)
    gpi = gpos.astype(np.int32).reshape(4, 1)

    shared = {
        "wq": Wq_p, "wk": Wk_pd, "wv": np.ascontiguousarray(Wv),
        "wo": np.ascontiguousarray(Wo), "gwf": gwf, "ewt": ewt,
        "cf": cf, "cb": cb, "ttok": tt, "brows": brows,
        "gpr": gpr, "gpc": gpc, "gpi": gpi,
    }
    in_maps = []
    for b in range(8):
        m = dict(shared)
        m["xfm"] = np.ascontiguousarray(x[b].T)
        m["xstd"] = np.ascontiguousarray(x[b])
        in_maps.append(m)
    return in_maps


def kernel(**inputs):
    in_maps = prepare_inputs(inputs)
    nc = build_program()
    res = run_bass_kernel_spmd(nc, in_maps, core_ids=list(range(8)))
    out = np.stack([np.asarray(r["out"], dtype=np.float32) for r in res.results])
    return out


# version 31 (final)
# speedup vs baseline: 1.0091x; 1.0091x over previous
import numpy as np
import ml_dtypes
import concourse.bass as bass
import concourse.mybir as mybir
from concourse.bass_utils import run_bass_kernel_spmd
from concourse.bass_core_rust import IndirectOffsetOnAxis
from contextlib import ExitStack

F32 = mybir.dt.float32
BF16 = mybir.dt.bfloat16
AF = mybir.ActivationFunctionType
ALU = mybir.AluOpType
AX = mybir.AxisListType.X
BF = ml_dtypes.bfloat16

L = 2048
D = 1024
H, G, HD = 16, 4, 64
WIN = 256
NT = 16
NC = 8
NP = 8

_l = np.arange(64)
_b, _r = _l // 32, _l % 32
PERM64 = np.where(_r < 16, 2 * (16 * _b + _r), 2 * (16 * _b + (_r - 16)) + 1)
SGN64 = np.where(_r < 16, -1.0, 1.0).astype(np.float32)
SHUF_MASK = [(i + 16) % 32 for i in range(32)]

CF_W = 176
C_IOTAC, C_BQ, C_BK, C_BO = 0, 16, 24, 28
C_ONESF, C_EPS, C_LC, C_LCT = 36, 164, 165, 169
CB_W = 4736
B_IDEN, B_ONES, B_MASKL, B_MASKR, B_ISHIFT, B_RCOS, B_RSIN = 0, 128, 256, 384, 512, 640, 2688

GB_EQO, GB_GCM, GB_EGC, GB_VGA, GB_XPV, GB_TGL = 0, 2048, 4096, 6144, 6404, 6469
GB_W = 6600
GB_S0 = GB_EGC
GB_S1 = GB_EGC + 1024

ES_SW = 644
FFN_NT_ES = 10

def strip_slots(tj):
    s0 = max(0, 2 - tj)
    s1 = 5 - max(0, tj + 2 - (NT - 1))
    return s0, s1


def pv_terms(qc):
    out = []
    for tj in range(max(0, 4 * qc - 2), min(NT, 4 * qc + 6)):
        qtl = max(tj - 2, 4 * qc, 0)
        qth = min(tj + 2, 4 * qc + 3, NT - 1)
        if qtl > qth:
            continue
        out.append((tj, qtl - (tj - 2), qth - qtl + 1, qtl))
    return out


def _check_geometry():
    seen = set()
    for qc in range(4):
        for tj, s0, n, qt0 in pv_terms(qc):
            for i in range(n):
                qt = qt0 + i
                s = s0 + i
                assert qt == tj - 2 + s and 0 <= qt < NT and abs(qt - tj) <= 2
                assert (tj, qt) not in seen
                seen.add((tj, qt))
    want = {(tj, qt) for tj in range(NT) for qt in range(NT) if abs(qt - tj) <= 2}
    assert seen == want


_check_geometry()


class Sched:
    ENGS = ("sync", "tensor", "vector", "scalar", "gpsimd")
    NDMA = 16

    def __init__(self, nc, stack):
        self.nc = nc
        self.sem = {e: stack.enter_context(nc.semaphore(f"sem_{e}")) for e in self.ENGS}
        self.dsem = [stack.enter_context(nc.semaphore(f"dma_{i}"))
                     for i in range(self.NDMA)]
        self.cnt = {e: 0 for e in self.ENGS}
        self.ops = {e: [] for e in self.ENGS}
        self.wm = {e: {} for e in self.ENGS}
        self.last_w = {}
        self.readers = {}
        self.dma_idx = 0
        self.stack = stack
        self.swsems = []

    def op(self, eng, fn, reads=(), writes=(), dma=False, swdge=False):
        need = {}

        def add(tok, c):
            if tok == ("e", "tensor") and eng == "tensor" and not dma:
                return
            if need.get(tok, 0) < c:
                need[tok] = c

        for k in reads:
            w = self.last_w.get(k)
            if w:
                add(*w)
        for k in writes:
            w = self.last_w.get(k)
            if w:
                add(*w)
            for tok, c in self.readers.get(k, {}).items():
                add(tok, c)
        if swdge:
            sem = self.stack.enter_context(
                self.nc.semaphore(f"swdma_{len(self.swsems)}"))
            self.swsems.append(sem)
            token = ("w", len(self.swsems) - 1)
            after = 16
            sem_inc = (sem, 16)
        elif dma:
            k_idx = self.dma_idx
            self.dma_idx += 1
            slot, rnd = k_idx % self.NDMA, k_idx // self.NDMA
            if rnd > 0:
                add(("d", slot), 16 * rnd)
            token = ("d", slot)
            after = 16 * (rnd + 1)
            sem_inc = (self.dsem[slot], 16)
        else:
            self.cnt[eng] += 1
            token = ("e", eng)
            after = self.cnt[eng]
            sem_inc = (self.sem[eng], 1)
        waits = []
        for tok, c in sorted(need.items()):
            if self.wm[eng].get(tok, 0) < c:
                sm = (self.sem[tok[1]] if tok[0] == "e"
                      else self.swsems[tok[1]] if tok[0] == "w"
                      else self.dsem[tok[1]])
                waits.append((sm, c))
                self.wm[eng][tok] = c
        self.ops[eng].append((waits, fn, sem_inc))
        for k in reads:
            d = self.readers.setdefault(k, {})
            if d.get(token, 0) < after:
                d[token] = after
        for k in writes:
            self.last_w[k] = (token, after)
            self.readers[k] = {}

    def emit(self, block):
        sched = self

        def section(e):
            def _(eng):
                for waits, fn, (sem, inc) in sched.ops[e]:
                    for (sm, c) in waits:
                        eng.wait_ge(sm, c)
                    fn(eng).then_inc(sem, inc)
                if e == "sync":
                    for i in range(sched.NDMA):
                        if sched.dma_idx > i:
                            tot = 16 * ((sched.dma_idx - 1 - i) // sched.NDMA + 1)
                            eng.wait_ge(sched.dsem[i], tot)
                    for sm in sched.swsems:
                        eng.wait_ge(sm, 16)
            return _

        for e in self.ENGS:
            getattr(block, e)(section(e))


def seg_keys(name, lo, hi):
    return [(name, s) for s in range(lo // 512, (hi + 511) // 512)]


def psum_keys(nc, t):
    b0 = nc.lookup_mloc(t).bank
    fb = int(np.prod(t.shape[1:])) * (4 if t.dtype == F32 else 2)
    nb = (fb + 2047) // 2048
    return [("pb", b0 + i) for i in range(nb)]


def build_program(debug=False):
    nc = bass.Bass("TRN2", target_bir_lowering=False)
    stack = ExitStack()

    xfm = nc.dram_tensor("xfm", [D, L], F32, kind="ExternalInput")
    xstd = nc.dram_tensor("xstd", [L, D], F32, kind="ExternalInput")
    wq_d = nc.dram_tensor("wq", [D, D], F32, kind="ExternalInput")
    wk_d = nc.dram_tensor("wk", [D, 512], F32, kind="ExternalInput")
    wv_d = nc.dram_tensor("wv", [D, 256], F32, kind="ExternalInput")
    wo_d = nc.dram_tensor("wo", [D, D], F32, kind="ExternalInput")
    gw_d = nc.dram_tensor("gwf", [128, 64], F32, kind="ExternalInput")
    ew_d = nc.dram_tensor("ewt", [8, D, D], F32, kind="ExternalInput")
    cf_d = nc.dram_tensor("cf", [128, CF_W], F32, kind="ExternalInput")
    cb_d = nc.dram_tensor("cb", [128, CB_W], BF16, kind="ExternalInput")
    tt_d = nc.dram_tensor("ttok", [128, 2048], BF16, kind="ExternalInput")
    br_d = nc.dram_tensor("brows", [1, 2048], F32, kind="ExternalInput")
    gpr_d = nc.dram_tensor("gpr", [128, 4], F32, kind="ExternalInput")
    gpc_d = nc.dram_tensor("gpc", [4, 1], F32, kind="ExternalInput")
    gpi_d = nc.dram_tensor("gpi", [4, 1], mybir.dt.int32, kind="ExternalInput")
    out_d = nc.dram_tensor("out", [L, D], F32, kind="ExternalOutput")
    dbg = {}
    if debug:
        for name, shp, dt in [
            ("dbg_kb", [128, 8192], BF16), ("dbg_qp", [128, 4096], BF16),
            ("dbg_es", [128, 10304], BF16), ("dbg_at", [128, 16384], BF16),
            ("dbg_vb", [128, 4160], BF16), ("dbg_xb", [128, 16384], BF16),
            ("dbg_ms", [128, 384], F32), ("dbg_g4b", [4, GB_W], BF16),
            ("dbg_em", [128, 68], BF16), ("dbg_eqt", [128, 64], BF16),
            ("dbg_xgf", [128, 32], BF16), ("dbg_kgf", [128, 16], BF16),
        ]:
            dbg[name] = nc.dram_tensor(name, shp, dt, kind="ExternalOutput")

    CF = nc.alloc_sbuf_tensor("CF", [128, CF_W], F32)
    CB = nc.alloc_sbuf_tensor("CB", [128, CB_W], BF16)
    XB = nc.alloc_sbuf_tensor("XB", [128, 16384], BF16)
    KB = nc.alloc_sbuf_tensor("KB", [128, 8192], BF16)
    VB = nc.alloc_sbuf_tensor("VB", [128, 4160], BF16)
    AT = nc.alloc_sbuf_tensor("AT", [128, 16384], BF16)
    QP = nc.alloc_sbuf_tensor("QP", [128, 4096], BF16)
    ES = nc.alloc_sbuf_tensor("ES", [128, 10304], BF16)
    WQ = nc.alloc_sbuf_tensor("WQ", [128, 8192], BF16)
    WK = nc.alloc_sbuf_tensor("WK", [128, 4096], BF16)
    WV = nc.alloc_sbuf_tensor("WV", [128, 2048], BF16)
    GWS = nc.alloc_sbuf_tensor("GWS", [128, 64], BF16)
    SC = nc.alloc_sbuf_tensor("SC", [128, 4096], F32)
    RS = nc.alloc_sbuf_tensor("RS", [128, 2560], BF16)
    ZR = nc.alloc_sbuf_tensor("ZR", [65, 2048], F32)
    MS = nc.alloc_sbuf_tensor("MS", [128, 384], F32)
    G4F = nc.alloc_sbuf_tensor("G4F", [4, 1024], F32)
    G4B = nc.alloc_sbuf_tensor("G4B", [4, GB_W], BF16)
    XGF = nc.alloc_sbuf_tensor("XGF", [128, 32], BF16)
    KGF = nc.alloc_sbuf_tensor("KGF", [128, 16], BF16)
    EM = nc.alloc_sbuf_tensor("EM", [128, 68], BF16)
    EQT = nc.alloc_sbuf_tensor("EQT", [128, 64], BF16)
    GPR = nc.alloc_sbuf_tensor("GPR", [128, 4], F32)
    GPC = nc.alloc_sbuf_tensor("GPC", [4, 1], F32)
    GPI = nc.alloc_sbuf_tensor("GPI", [4, 1], mybir.dt.int32)
    BRB = nc.alloc_sbuf_tensor("BRB", [1, 2048], BF16)

    sch = Sched(nc, stack)
    A = sch.op

    KCF, KCB = [("cf", 0)], [("cb", 0)]
    KBR = [("br", 0)]

    def dma(dst_ap, src_ap, reads, writes):
        A("sync", lambda e, d=dst_ap, s=src_ap: e.dma_start(d, s),
          reads=reads, writes=writes, dma=True)

    if debug:
        for t, nm, w in [(ES, "es", 10304), (QP, "qp", 4096), (RS, "rs", 2560),
                         (KB, "kb", 8192), (MS, "ms", 384), (G4B, "g4b", GB_W),
                         (VB, "vb", 4160), (AT, "at", 16384), (XB, "xb", 16384),
                         (EM, "em", 68), (EQT, "eqt", 64), (XGF, "xgf", 32),
                         (KGF, "kgf", 16), (WQ, "wq", 8192), (SC, "sc", 4096)]:
            A("vector", lambda e, t=t, w=w: e.memset(t[:, 0:w], 0.0),
              writes=seg_keys(nm, 0, w))
        A("vector", lambda e: e.memset(ZR[:], 0.0), writes=[("zr", 0), ("zr", 1)])
    dma(CF[:], cf_d[:], [], KCF)
    dma(CB[:], cb_d[:], [], KCB)
    dma(AT[:, 0:2048], tt_d[:], [], seg_keys("at", 0, 2048))
    dma(GPR[:], gpr_d[:], [], [("gpr", 0)])
    dma(GPC[:], gpc_d[:], [], [("gpc", 0)])
    dma(GPI[:], gpi_d[:], [], [("gpi", 0)])
    dma(SC[0:1, 0:2048], br_d[:], [], seg_keys("sc", 0, 2048))
    A("vector", lambda e: e.tensor_copy(BRB[:], SC[0:1, 0:2048]),
      reads=seg_keys("sc", 0, 2048), writes=[("brb", 0)])

    gstack = ExitStack()
    ip_tr = gstack.enter_context(nc.psum_tensor("ip_tr", [128, 512], BF16))
    ip_g1 = gstack.enter_context(nc.psum_tensor("ip_g1", [4, 512], F32))
    ip_g2 = gstack.enter_context(nc.psum_tensor("ip_g2", [4, 512], F32))
    ip_oc = gstack.enter_context(nc.psum_tensor("ip_oc", [128, 512], F32))
    KTR, KG1 = psum_keys(nc, ip_tr), psum_keys(nc, ip_g1)
    KG2, KOC = psum_keys(nc, ip_g2), psum_keys(nc, ip_oc)
    KMS = [("ms", 0)]
    KEM, KEQT, KG4B, KG4F = [("em", 0)], [("eqt", 0)], [("g4b", 0)], [("g4f", 0)]
    A("vector", lambda e: e.tensor_scalar(MS[0:4, 156:160], GPR[0:4, :], GPC[0:4, :],
                                          None, ALU.subtract),
      reads=[("gpr", 0), ("gpc", 0)], writes=KMS)
    A("vector", lambda e: e.tensor_scalar(MS[0:4, 160:164], MS[0:4, 156:160], 0.0,
                                          None, ALU.is_equal),
      reads=KMS, writes=KMS)
    A("vector", lambda e: e.tensor_tensor(MS[0:4, 164:168], MS[0:4, 160:164],
                                          CF[0:4, C_LC:C_LC + 4], ALU.mult),
      reads=KMS + KCF, writes=KMS)
    A("vector", lambda e: e.reduce_sum(MS[0:4, 168:169], MS[0:4, 164:168], axis=AX),
      reads=KMS, writes=KMS)
    A("vector", lambda e: e.tensor_scalar(MS[0:4, 169:170], MS[0:4, 168:169], 0.0,
                                          None, ALU.is_equal),
      reads=KMS, writes=KMS)
    A("vector", lambda e: e.tensor_tensor(MS[0:4, 170:174], MS[0:4, 160:164],
                                          CF[0:4, C_LCT:C_LCT + 4], ALU.mult),
      reads=KMS + KCF, writes=KMS)
    A("vector", lambda e: e.tensor_copy(RS[0:4, 0:4], MS[0:4, 170:174]),
      reads=KMS, writes=seg_keys("rs", 0, 4))
    A("tensor", lambda e: e.matmul(ip_g1[0:1, 0:4], CB[0:4, B_ONES:B_ONES + 1],
                                   RS[0:4, 0:4], start=True, stop=True),
      reads=seg_keys("rs", 0, 4) + KCB, writes=KG1)
    A("vector", lambda e: e.tensor_scalar(MS[0:1, 174:178], ip_g1[0:1, 0:4], 0.0,
                                          None, ALU.is_equal),
      reads=KG1, writes=KMS)
    A("vector", lambda e: e.tensor_copy(RS[0:1, 4:8], MS[0:1, 174:178]),
      reads=KMS, writes=seg_keys("rs", 0, 8))
    A("tensor", lambda e: e.matmul(ip_oc[:, 0:4], CB[0:1, B_ONES:B_ONES + 128],
                                   RS[0:1, 4:8], start=True, stop=True),
      reads=seg_keys("rs", 0, 8) + KCB, writes=KOC)
    A("vector", lambda e: e.tensor_copy(EM[:, 64:68], ip_oc[:, 0:4]),
      reads=KOC, writes=KEM)

    g0 = GPR[:, 0:4]
    gprb = bass.AP(g0.tensor, g0.offset, [g0.ap[0], [0, NT], [1, 4]])
    c0 = CF[:, C_IOTAC:C_IOTAC + NT]
    iotb = bass.AP(c0.tensor, c0.offset, [c0.ap[0], [1, NT], [0, 4]])
    A("vector", lambda e: e.tensor_tensor(
        MS[:, 176:240].rearrange("p (a b) -> p a b", b=4), gprb, iotb, ALU.subtract),
      reads=[("gpr", 0)] + KCF, writes=KMS)
    A("vector", lambda e: e.tensor_scalar(EQT[:, 0:64], MS[:, 176:240], 0.0,
                                          None, ALU.is_equal),
      reads=KMS, writes=KEQT)
    A("vector", lambda e: e.tensor_scalar(MS[:, 240:304], MS[:, 176:240],
                                          float(WIN), None, ALU.is_gt),
      reads=KMS, writes=KMS)
    A("vector", lambda e: e.tensor_scalar(MS[:, 304:368], MS[:, 176:240],
                                          float(-WIN), None, ALU.is_lt),
      reads=KMS, writes=KMS)
    A("vector", lambda e: e.tensor_tensor(MS[:, 240:304], MS[:, 240:304],
                                          MS[:, 304:368], ALU.add),
      reads=KMS, writes=KMS)
    e0 = EQT[:, 0:64]
    eqs = [bass.AP(e0.tensor, e0.offset + j, [e0.ap[0], [4, NT]]) for j in range(4)]
    A("vector", lambda e: e.tensor_tensor(MS[:, 368:384], eqs[0], eqs[1], ALU.add),
      reads=KEQT, writes=KMS)
    A("vector", lambda e: e.tensor_tensor(MS[:, 368:384], MS[:, 368:384], eqs[2],
                                          ALU.add),
      reads=KMS + KEQT, writes=KMS)
    A("vector", lambda e: e.tensor_tensor(MS[:, 368:384], MS[:, 368:384], eqs[3],
                                          ALU.add),
      reads=KMS + KEQT, writes=KMS)
    A("vector", lambda e: e.tensor_scalar(MS[:, 368:384], MS[:, 368:384], 1.0,
                                          None, ALU.min),
      reads=KMS, writes=KMS)
    A("vector", lambda e: e.tensor_scalar(MS[:, 368:384], MS[:, 368:384], -1.0, 1.0,
                                          ALU.mult, ALU.add),
      reads=KMS, writes=KMS)
    n0 = MS[:, 368:384]
    notb = bass.AP(n0.tensor, n0.offset, [n0.ap[0], [1, NT], [0, 4]])
    A("vector", lambda e: e.tensor_tensor(
        MS[:, 240:304].rearrange("p (a b) -> p a b", b=4),
        MS[:, 240:304].rearrange("p (a b) -> p a b", b=4), notb, ALU.mult),
      reads=KMS, writes=KMS)
    o0 = EM[:, 64:68]
    occb = bass.AP(o0.tensor, o0.offset, [o0.ap[0], [0, NT], [1, 4]])
    A("vector", lambda e: e.tensor_tensor(
        EM[:, 0:64].rearrange("p (a b) -> p a b", b=4),
        MS[:, 240:304].rearrange("p (a b) -> p a b", b=4), occb, ALU.mult),
      reads=KMS + KEM, writes=KEM)

    kzr_s = [("zr", 0), ("zr", 1)] + [("zrc", qc) for qc in range(4)]
    A("gpsimd", lambda e: e.iota(ZR[0:4, 0:2048], [[1, 2048]], base=0,
                                 channel_multiplier=0,
                                 allow_small_or_imprecise_dtypes=True),
      writes=kzr_s)
    A("vector", lambda e: e.tensor_scalar(ZR[0:4, 0:2048], ZR[0:4, 0:2048],
                                          GPC[0:4, :], None, ALU.subtract),
      reads=kzr_s + [("gpc", 0)], writes=kzr_s)
    A("vector", lambda e: e.tensor_scalar(G4B[0:4, GB_EQO:GB_EQO + 2048],
                                          ZR[0:4, 0:2048], 0.0, None, ALU.is_equal),
      reads=kzr_s, writes=KG4B)
    A("vector", lambda e: e.tensor_scalar(G4B[0:4, GB_EQO:GB_EQO + 2048],
                                          G4B[0:4, GB_EQO:GB_EQO + 2048],
                                          MS[0:4, 169:170], None, ALU.mult),
      reads=KG4B + KMS, writes=KG4B)
    A("vector", lambda e: e.tensor_scalar(G4B[0:4, GB_GCM:GB_GCM + 2048],
                                          ZR[0:4, 0:2048], float(WIN), None,
                                          ALU.is_gt),
      reads=kzr_s, writes=KG4B)
    A("vector", lambda e: e.tensor_scalar(ZR[0:4, 0:2048], ZR[0:4, 0:2048],
                                          float(-WIN), None, ALU.is_lt),
      reads=kzr_s, writes=kzr_s)
    A("vector", lambda e: e.tensor_tensor(G4B[0:4, GB_GCM:GB_GCM + 2048],
                                          G4B[0:4, GB_GCM:GB_GCM + 2048],
                                          ZR[0:4, 0:2048], ALU.add),
      reads=KG4B + kzr_s, writes=KG4B)
    A("vector", lambda e: e.tensor_scalar(G4B[0:4, GB_GCM:GB_GCM + 2048],
                                          G4B[0:4, GB_GCM:GB_GCM + 2048],
                                          MS[0:4, 169:170], None, ALU.mult),
      reads=KG4B + KMS, writes=KG4B)

    A("gpsimd", lambda e: e.indirect_dma_start(
        G4F[0:4, 0:1024], None, xstd[:, :],
        IndirectOffsetOnAxis(ap=GPI[:], axis=0)),
      reads=[("gpi", 0)], writes=KG4F, dma=True, swdge=True)
    A("vector", lambda e: e.tensor_copy(G4B[0:4, GB_S0:GB_S0 + 1024],
                                        G4F[0:4, 0:1024]),
      reads=KG4F, writes=KG4B)

    def transpose4(src_off, n_chunks, dst, dst_name, extra_reads):
        kst = seg_keys("at", 3072, 4096)
        A("vector", lambda e: e.memset(AT[:, 3072:4096], 0.0), writes=kst)
        A("vector", lambda e, so=src_off, n=n_chunks: e.tensor_copy(
            AT[0:4, 3072:3072 + 128 * n], G4B[0:4, so:so + 128 * n]),
          reads=KG4B + extra_reads, writes=kst)
        for cc in range(n_chunks):
            A("tensor", lambda e, cc=cc: e.transpose(
                ip_tr[:, 0:128], AT[:, 3072 + 128 * cc:3072 + 128 * (cc + 1)],
                CB[:, B_IDEN:B_IDEN + 128]),
              reads=kst + KCB, writes=KTR)
            A("vector", lambda e, cc=cc, dst=dst: e.tensor_copy(
                dst[:, 4 * cc:4 * (cc + 1)], ip_tr[:, 0:4]),
              reads=KTR, writes=[(dst_name, 0)])

    transpose4(GB_S0, 8, XGF, "xgf", [])

    for t in range(NT):
        A("tensor", lambda e, t=t: e.matmul(
            ip_g1[0:4, 0:128], EQT[:, 4 * t:4 * t + 4],
            AT[:, 128 * t:128 * (t + 1)], start=(t == 0), stop=(t == NT - 1)),
          reads=seg_keys("at", 128 * t, 128 * (t + 1)) + KEQT, writes=KG1)
    A("vector", lambda e: e.tensor_copy(G4B[0:4, GB_TGL:GB_TGL + 128],
                                        ip_g1[0:4, 0:128]),
      reads=KG1, writes=KG4B)

    wslot = [0]

    def load_weight(dram_ap_fn, nchunk, dst, dst_name, dst_off, width, eng="gpsimd"):
        for ci in range(nchunk):
            sl = (wslot[0] % 4) * 1024
            wslot[0] += 1
            ksc = seg_keys("sc", sl, sl + width)
            dma(SC[:, sl:sl + width], dram_ap_fn(ci), [], ksc)
            kd = seg_keys(dst_name, dst_off + ci * width, dst_off + (ci + 1) * width)
            A(eng, lambda e, ci=ci, sl=sl, dst=dst, w=width, o=dst_off: e.tensor_copy(
                dst[:, o + ci * w:o + (ci + 1) * w], SC[:, sl:sl + w]),
              reads=ksc, writes=kd)

    load_weight(lambda ci: wk_d[128 * ci:128 * (ci + 1), :], 8, WK, "wk", 0, 512)
    load_weight(lambda ci: wv_d[128 * ci:128 * (ci + 1), :], 8, WV, "wv", 0, 256)

    for c in range(NC):
        sl = 1024 * (wslot[0] % 4)
        wslot[0] += 2
        sl = 2048 * ((c % 2))
        ksc = seg_keys("sc", sl, sl + 2048)
        dma(SC[:, sl:sl + 2048], xfm[128 * c:128 * (c + 1), :], [], ksc)
        A("vector", lambda e, c=c, sl=sl: e.tensor_copy(
            XB[:, 2048 * c:2048 * (c + 1)], SC[:, sl:sl + 2048]),
          reads=ksc, writes=seg_keys("xb", 2048 * c, 2048 * (c + 1)))

    load_weight(lambda ci: wq_d[128 * ci:128 * (ci + 1), :], 8, WQ, "wq", 0, 1024)
    load_weight(lambda ci: gw_d[:, :], 1, GWS, "gw", 0, 64)

    with nc.psum_tensor("ip_kv", [128, 512], F32) as ip_kv, \
         nc.psum_tensor("ip_kv2", [128, 512], F32) as ip_kv2:
        KKV = psum_keys(nc, ip_kv)
        KKV2 = psum_keys(nc, ip_kv2)

        def rope_evac(kpsum, psum_ap, bias_col, scale, dst_ap, dst_keys, cosc, sinc):
            kr = seg_keys("rs", 0, 2048)
            if scale is None:
                A("vector", lambda e: e.tensor_scalar(
                    RS[:, 0:512], psum_ap, bias_col, None, ALU.add),
                  reads=kpsum + KCF, writes=kr)
            else:
                A("vector", lambda e: e.tensor_scalar(
                    RS[:, 0:512], psum_ap, bias_col, scale, ALU.add, ALU.mult),
                  reads=kpsum + KCF, writes=kr)
            A("vector", lambda e: e.stream_shuffle(RS[:, 512:1024], RS[:, 0:512], SHUF_MASK),
              reads=kr, writes=kr)
            A("vector", lambda e, cosc=cosc: e.tensor_tensor(
                RS[:, 1024:1536], RS[:, 0:512], CB[:, cosc:cosc + 512], ALU.mult),
              reads=kr + KCB, writes=kr)
            A("vector", lambda e, sinc=sinc: e.tensor_tensor(
                RS[:, 1536:2048], RS[:, 512:1024], CB[:, sinc:sinc + 512], ALU.mult),
              reads=kr + KCB, writes=kr)
            A("vector", lambda e, dst_ap=dst_ap: e.tensor_tensor(
                dst_ap, RS[:, 1024:1536], RS[:, 1536:2048], ALU.add),
              reads=kr, writes=dst_keys)

        for kc in range(4):
            for qc in range(4):
                kv, kkv = (ip_kv, KKV) if (4 * kc + qc) % 2 == 0 else (ip_kv2, KKV2)
                for ci in range(NC):
                    A("tensor", lambda e, kc=kc, qc=qc, ci=ci, kv=kv: e.matmul(
                        kv[:, 0:512], WK[:, 512 * ci + 128 * kc:512 * ci + 128 * (kc + 1)],
                        XB[:, 2048 * ci + 512 * qc:2048 * ci + 512 * (qc + 1)],
                        start=(ci == 0), stop=(ci == NC - 1)),
                      reads=seg_keys("wk", 512 * ci, 512 * (ci + 1)) +
                            seg_keys("xb", 2048 * ci + 512 * qc, 2048 * ci + 512 * (qc + 1)),
                      writes=kkv)
                rope_evac(kkv, kv[:, 0:512], CF[:, C_BK + kc:C_BK + kc + 1], None,
                          KB[:, 2048 * kc + 512 * qc:2048 * kc + 512 * (qc + 1)],
                          seg_keys("kb", 2048 * kc + 512 * qc, 2048 * kc + 512 * (qc + 1)),
                          B_RCOS + 512 * qc, B_RSIN + 512 * qc)

        A("vector", lambda e: e.memset(VB[:], 1.0), writes=seg_keys("vb", 0, 4160))
        for t in range(NT):
            kv, kkv = (ip_kv, KKV) if t % 2 == 0 else (ip_kv2, KKV2)
            for ci in range(NC):
                A("tensor", lambda e, t=t, ci=ci, kv=kv: e.matmul(
                    kv[:, 0:256], XB[:, 2048 * ci + 128 * t:2048 * ci + 128 * (t + 1)],
                    WV[:, 256 * ci:256 * (ci + 1)], start=(ci == 0), stop=False),
                  reads=seg_keys("wv", 256 * ci, 256 * (ci + 1)) +
                        seg_keys("xb", 2048 * ci + 128 * t, 2048 * ci + 128 * (t + 1)),
                  writes=kkv)
            A("tensor", lambda e, kv=kv: e.matmul(
                kv[:, 0:256], CB[0:1, B_ONES:B_ONES + 128], BRB[0:1, 1536:1792],
                start=False, stop=True),
              reads=[("brb", 0)] + KCB, writes=kkv)
            for g in range(4):
                A("vector", lambda e, t=t, g=g, kv=kv: e.tensor_copy(
                    VB[:, 260 * t + 65 * g:260 * t + 65 * g + 64],
                    kv[:, 64 * g:64 * (g + 1)]),
                  reads=kkv, writes=seg_keys("vb", 260 * t, 260 * (t + 1)))

        def glob_proj(w_sb, w_name, w_cw, dout, bias_off, scale, dst_off):
            nhalf = (dout + 511) // 512
            for half in range(nhalf):
                w0 = 512 * half
                wd = min(512, dout - w0)
                ps = ip_g1 if half == 0 else ip_g2
                kps = KG1 if half == 0 else KG2
                for ci in range(NC):
                    A("tensor", lambda e, ci=ci, ps=ps, w0=w0, wd=wd, w_sb=w_sb: e.matmul(
                        ps[0:4, 0:wd], XGF[:, 4 * ci:4 * (ci + 1)],
                        w_sb[:, w_cw * ci + w0:w_cw * ci + w0 + wd],
                        start=(ci == 0), stop=False),
                      reads=[("xgf", 0)] + seg_keys(w_name, w_cw * ci, w_cw * (ci + 1)),
                      writes=kps)
                A("tensor", lambda e, ps=ps, w0=w0, wd=wd: e.matmul(
                    ps[0:4, 0:wd], CB[0:1, B_ONES:B_ONES + 4],
                    BRB[0:1, bias_off + w0:bias_off + w0 + wd], start=False, stop=True),
                  reads=[("brb", 0)] + KCB, writes=kps)
                A("vector", lambda e, ps=ps, w0=w0, wd=wd: e.tensor_scalar(
                    G4F[0:4, dst_off + w0:dst_off + w0 + wd], ps[0:4, 0:wd],
                    scale, None, ALU.mult),
                  reads=kps, writes=KG4F)

        def glob_rope(src_f32_off, dout, dst_bf_off):
            nu = dout // 64
            A("vector", lambda e: e.tensor_copy(
                G4B[0:4, dst_bf_off:dst_bf_off + dout].rearrange("p (u w) -> p u w", w=64),
                G4F[0:4, src_f32_off:src_f32_off + dout].rearrange("p (u w) -> p u w", w=64)),
              reads=KG4F, writes=KG4B)
            src = G4B[0:4, dst_bf_off:dst_bf_off + dout]
            sw_in = bass.AP(src.tensor, src.offset + 16,
                            [src.ap[0], [32, dout // 32], [-16, 2], [1, 16]])
            A("vector", lambda e: e.tensor_copy(
                G4B[0:4, GB_S1:GB_S1 + dout].rearrange("p (a b c) -> p a b c", b=2, c=16),
                sw_in),
              reads=KG4B, writes=KG4B)
            cosap = G4B[0:4, GB_TGL:GB_TGL + 64]
            sinap = G4B[0:4, GB_TGL + 64:GB_TGL + 128]
            cosb = bass.AP(cosap.tensor, cosap.offset, [cosap.ap[0], [0, nu], [1, 64]])
            sinb = bass.AP(sinap.tensor, sinap.offset, [sinap.ap[0], [0, nu], [1, 64]])
            v3 = G4B[0:4, dst_bf_off:dst_bf_off + dout].rearrange("p (u w) -> p u w", w=64)
            s3 = G4B[0:4, GB_S1:GB_S1 + dout].rearrange("p (u w) -> p u w", w=64)
            A("vector", lambda e: e.tensor_tensor(v3, v3, cosb, ALU.mult),
              reads=KG4B, writes=KG4B)
            A("vector", lambda e: e.tensor_tensor(s3, s3, sinb, ALU.mult),
              reads=KG4B, writes=KG4B)
            A("vector", lambda e: e.tensor_tensor(v3, v3, s3, ALU.add),
              reads=KG4B, writes=KG4B)

        glob_proj(WK, "wk", 512, 512, 1024, 1.0, 0)
        glob_rope(0, 512, GB_S0)
        transpose4(GB_S0, 4, KGF, "kgf", [])
        for ci in range(NC):
            A("tensor", lambda e, ci=ci: e.matmul(
                ip_g1[0:4, 0:256], XGF[:, 4 * ci:4 * (ci + 1)],
                WV[:, 256 * ci:256 * (ci + 1)], start=(ci == 0), stop=False),
              reads=[("xgf", 0)] + seg_keys("wv", 256 * ci, 256 * (ci + 1)), writes=KG1)
        A("tensor", lambda e: e.matmul(
            ip_g1[0:4, 0:256], CB[0:1, B_ONES:B_ONES + 4],
            BRB[0:1, 1536:1792], start=False, stop=True),
          reads=[("brb", 0)] + KCB, writes=KG1)
        A("vector", lambda e: e.memset(G4B[0:4, GB_VGA:GB_VGA + 260], 1.0),
          writes=KG4B)
        for g in range(4):
            A("vector", lambda e, g=g: e.tensor_copy(
                G4B[0:4, GB_VGA + 65 * g:GB_VGA + 65 * g + 64],
                ip_g1[0:4, 64 * g:64 * (g + 1)]),
              reads=KG1, writes=KG4B)

        glob_proj(WQ, "wq", 1024, 1024, 0, 0.125, 0)
        glob_rope(0, 1024, GB_S0)
        transpose4(GB_S0, 8, XGF, "xgf", [])

    gstack.close()

    with nc.psum_tensor("ap_nm", [65, 2048], F32) as ap_nm, \
         nc.psum_tensor("ap_s0", [128, 768], F32) as ap_s0, \
         nc.psum_tensor("ap_s1", [128, 768], F32) as ap_s1:
        KNM = psum_keys(nc, ap_nm)
        SB2 = [ap_s0, ap_s1]
        KSC2 = [psum_keys(nc, ap_s0), psum_keys(nc, ap_s1)]
        KNMq = [[KNM[qc]] for qc in range(4)]

        def strip_region(h, tj):
            if tj >= 8:
                return ES, 5152 + ES_SW * (tj - 8), "es"
            if h % 2 == 0:
                return ES, ES_SW * tj, "es"
            if tj < 6:
                return WK, ES_SW * tj, "wk"
            return WV, ES_SW * (tj - 6), "wv"

        def strip_runs(h, t0, t1):
            runs = []
            for tj in range(t0, t1 + 1):
                t_, b_, k_ = strip_region(h, tj)
                if runs and runs[-1][0] is t_ and \
                   runs[-1][1] + ES_SW * (tj - runs[-1][3][0]) == b_:
                    runs[-1][3].append(tj)
                else:
                    runs.append([t_, b_, k_, [tj]])
            return runs

        def emit_gc(h):
            g, hb = h // 4, h % 2
            base = 64 * hb
            slot = 2048 * ((h // 2) % 2)
            kqp = seg_keys("qp", slot, slot + 2048)
            for q2 in range(4):
                ps, kps = SB2[q2 % 2], KSC2[q2 % 2]
                A("tensor", lambda e, q2=q2, base=base, g=g, slot=slot, ps=ps: e.matmul(
                    ps[0:4, 0:512], KGF[base:base + 64, 4 * g:4 * (g + 1)],
                    QP[base:base + 64, slot + 512 * q2:slot + 512 * (q2 + 1)],
                    start=True, stop=True),
                  reads=[("kgf", 0)] + kqp, writes=kps)
                A("scalar", lambda e, q2=q2, ps=ps: e.activation(
                    G4B[0:4, GB_EGC + 512 * q2:GB_EGC + 512 * (q2 + 1)],
                    ps[0:4, 0:512], AF.Exp),
                  reads=kps, writes=KG4B)
                A("vector", lambda e, q2=q2: e.tensor_tensor(
                    G4B[0:4, GB_EGC + 512 * q2:GB_EGC + 512 * (q2 + 1)],
                    G4B[0:4, GB_EGC + 512 * q2:GB_EGC + 512 * (q2 + 1)],
                    G4B[0:4, GB_GCM + 512 * q2:GB_GCM + 512 * (q2 + 1)], ALU.mult),
                  reads=KG4B, writes=KG4B)

        def emit_strips(h, t0, t1):
            g, hb = h // 4, h % 2
            p = h // 2
            base = 64 * hb
            slot = 2048 * (p % 2)
            kqp = seg_keys("qp", slot, slot + 2048)
            for tj in range(t0, t1 + 1):
                s0, s1 = strip_slots(tj)
                ap_sc = ap_s0 if tj % 2 == 0 else ap_s1
                KSC = KSC2[tj % 2]
                est, esb, esk = strip_region(h, tj)
                kes = seg_keys(esk, esb, esb + ES_SW)
                kkb = seg_keys("kb", 2048 * g + 128 * tj, 2048 * g + 128 * (tj + 1))
                a_hi = min(s1, 4)
                A("tensor", lambda e, tj=tj, s0=s0, a_hi=a_hi, ap_sc=ap_sc,
                  base=base, g=g, slot=slot: e.matmul(
                    ap_sc[:, 128 * s0:128 * a_hi],
                    KB[base:base + 64, 2048 * g + 128 * tj:2048 * g + 128 * (tj + 1)],
                    QP[base:base + 64,
                       slot + 128 * (tj - 2 + s0):slot + 128 * (tj - 2 + a_hi)],
                    start=True, stop=True),
                  reads=kkb + kqp, writes=KSC)
                if s1 == 5:
                    A("tensor", lambda e, tj=tj, base=base, g=g, slot=slot,
                      ap_sc=ap_sc: e.matmul(
                        ap_sc[:, 512:640],
                        KB[base:base + 64, 2048 * g + 128 * tj:2048 * g + 128 * (tj + 1)],
                        QP[base:base + 64, slot + 128 * (tj + 2):slot + 128 * (tj + 3)],
                        start=True, stop=False),
                      reads=kkb + kqp, writes=KSC)
                A("tensor", lambda e, tj=tj, base=base, p=p, g=g, s1=s1,
                  ap_sc=ap_sc: e.matmul(
                    ap_sc[:, 640:644],
                    KB[base:base + 64, 2048 * g + 128 * tj:2048 * g + 128 * (tj + 1)],
                    XGF[base:base + 64, 4 * p:4 * (p + 1)],
                    start=(s1 != 5), stop=True),
                  reads=kkb + [("xgf", 0)], writes=KSC)
                lo = 128 * s0
                hi = 640 if s1 == 5 else 128 * a_hi
                kese = [("ese", esk, esb)]
                if hi == 640:
                    A("scalar", lambda e, lo=lo, ap_sc=ap_sc, est=est, esb=esb:
                      e.activation(est[:, esb + lo:esb + 644],
                                   ap_sc[:, lo:644], AF.Exp),
                      reads=KSC, writes=kes + kese)
                else:
                    A("scalar", lambda e, lo=lo, hi=hi, ap_sc=ap_sc, est=est, esb=esb:
                      e.activation(est[:, esb + lo:esb + hi],
                                   ap_sc[:, lo:hi], AF.Exp),
                      reads=KSC, writes=kes)
                    A("scalar", lambda e, ap_sc=ap_sc, est=est, esb=esb:
                      e.activation(est[:, esb + 640:esb + 644],
                                   ap_sc[:, 640:644], AF.Exp),
                      reads=KSC, writes=kese)

        def emit_masks(h):
            def masked(c0, c1, mask_off, t0, t1):
                for est, b0, k_, tjs in strip_runs(h, t0, t1):
                    n = len(tjs)
                    kk = seg_keys(k_, b0, b0 + ES_SW * n)
                    v = est[:, b0 + c0:b0 + c0 + 1]
                    vap = bass.AP(v.tensor, v.offset,
                                  [v.ap[0], [ES_SW, n], [1, c1 - c0]])
                    if mask_off is not None:
                        m = CB[:, mask_off:mask_off + 128]
                        map_ = bass.AP(m.tensor, m.offset,
                                       [m.ap[0], [0, n], [1, 128]])
                    else:
                        m = EM[:, 4 * tjs[0]:4 * tjs[0] + 1]
                        map_ = bass.AP(m.tensor, m.offset, [m.ap[0], [4, n], [1, 4]])
                    A("gpsimd", lambda e, vap=vap, map_=map_: e.tensor_tensor(
                        vap, vap, map_, ALU.mult),
                      reads=kk + KCB + KEM, writes=kk)
            masked(0, 128, B_MASKL, 2, 15)
            masked(512, 640, B_MASKR, 0, 13)
            for est, b0, k_, tjs in strip_runs(h, 0, 15):
                n = len(tjs)
                kk = [("ese", k_, b0 + ES_SW * i) for i in range(n)]
                v = est[:, b0 + 640:b0 + 641]
                vap = bass.AP(v.tensor, v.offset, [v.ap[0], [ES_SW, n], [1, 4]])
                m = EM[:, 4 * tjs[0]:4 * tjs[0] + 1]
                map_ = bass.AP(m.tensor, m.offset, [m.ap[0], [4, n], [1, 4]])
                A("gpsimd", lambda e, vap=vap, map_=map_: e.tensor_tensor(
                    vap, vap, map_, ALU.mult),
                  reads=kk + KEM, writes=kk)

        def emit_extra_pv(h):
            g = h // 4
            for tj in range(NT):
                est, esb, esk = strip_region(h, tj)
                A("tensor", lambda e, tj=tj, g=g, est=est, esb=esb: e.matmul(
                    ap_s0[0:4, 0:65], est[:, esb + 640:esb + 644],
                    VB[:, 260 * tj + 65 * g:260 * tj + 65 * (g + 1)],
                    start=(tj == 0), stop=(tj == NT - 1)),
                  reads=[("ese", esk, esb)] +
                        seg_keys("vb", 260 * tj, 260 * (tj + 1)),
                  writes=KSC2[0])
            A("vector", lambda e: e.tensor_copy(
                G4B[0:4, GB_XPV:GB_XPV + 65], ap_s0[0:4, 0:65]),
              reads=KSC2[0], writes=KG4B)

        def emit_pv_half(h, qh):
            g = h // 4
            for qc in (2 * qh, 2 * qh + 1):
                off = 512 * qc
                A("tensor", lambda e, qc=qc, off=off, g=g: e.matmul(
                    ap_nm[:, off:off + 512],
                    G4B[0:4, GB_VGA + 65 * g:GB_VGA + 65 * (g + 1)],
                    G4B[0:4, GB_EGC + 512 * qc:GB_EGC + 512 * (qc + 1)],
                    start=True, stop=False),
                  reads=KG4B, writes=KNMq[qc])
                for tj, sl0, ntl, qt0 in pv_terms(qc):
                    est, esb, esk = strip_region(h, tj)
                    A("tensor", lambda e, sl0=sl0, ntl=ntl, qt0=qt0, tj=tj,
                      g=g, est=est, esb=esb: e.matmul(
                        ap_nm[:, 128 * qt0:128 * (qt0 + ntl)],
                        VB[:, 260 * tj + 65 * g:260 * tj + 65 * (g + 1)],
                        est[:, esb + 128 * sl0:esb + 128 * (sl0 + ntl)],
                        start=False, stop=False),
                      reads=seg_keys(esk, esb, esb + ES_SW) +
                            seg_keys("vb", 260 * tj, 260 * (tj + 1)),
                      writes=KNMq[qc])
                A("tensor", lambda e, qc=qc, off=off: e.matmul(
                    ap_nm[:, off:off + 512],
                    G4B[0:4, GB_XPV:GB_XPV + 65],
                    G4B[0:4, GB_EQO + 512 * qc:GB_EQO + 512 * (qc + 1)],
                    start=False, stop=True),
                  reads=KG4B, writes=KNMq[qc])

        def emit_lnexp(h, qh):
            kzr = [("zr", qh)]
            A("scalar", lambda e, qh=qh: e.activation(
                ZR[64:65, 1024 * qh:1024 * (qh + 1)],
                ap_nm[64:65, 1024 * qh:1024 * (qh + 1)], AF.Ln),
              reads=KNMq[2 * qh] + KNMq[2 * qh + 1], writes=kzr)
            for qc in (2 * qh, 2 * qh + 1):
                rzo = 512 * qc
                krz = [("rzb", qc)] + seg_keys("rs", rzo, rzo + 512)
                A("scalar", lambda e, qc=qc, rzo=rzo: e.activation(
                    RS[64:65, rzo:rzo + 512], ZR[64:65, 512 * qc:512 * (qc + 1)],
                    AF.Exp, scale=-1.0),
                  reads=kzr, writes=krz)

        def emit_evac_rep(h, qh):
            for qc in (2 * qh, 2 * qh + 1):
                kzc = [("zrc", qc)]
                rzo = 512 * qc
                krz = [("rzb", qc)] + seg_keys("rs", rzo, rzo + 512)
                ps, kps = SB2[qc % 2], KSC2[qc % 2]
                A("tensor", lambda e, qc=qc, ps=ps, rzo=rzo: e.matmul(
                    ps[0:64, 0:512], CB[64:65, B_ONES:B_ONES + 64],
                    RS[64:65, rzo:rzo + 512], start=True, stop=True),
                  reads=krz + KCB, writes=kps)
                A("vector", lambda e, qc=qc, ps=ps: e.tensor_copy(
                    ZR[0:64, 512 * qc:512 * (qc + 1)], ps[0:64, 0:512]),
                  reads=kps, writes=kzc)

        def emit_evac_div(h, qh):
            p, hb = h // 2, h % 2
            for qc in (2 * qh, 2 * qh + 1):
                off = 512 * qc
                kzc = [("zrc", qc)]
                ps2, kps2 = SB2[1 - qc % 2], KSC2[1 - qc % 2]
                if hb == 0:
                    A("vector", lambda e, p=p, qc=qc, off=off: e.tensor_tensor(
                        AT[0:64, 2048 * p + 512 * qc:2048 * p + 512 * (qc + 1)],
                        ap_nm[0:64, off:off + 512],
                        ZR[0:64, 512 * qc:512 * (qc + 1)], ALU.mult),
                      reads=KNMq[qc] + kzc,
                      writes=seg_keys("at", 2048 * p + 512 * qc,
                                      2048 * p + 512 * (qc + 1)))
                else:
                    krs = [("rsb", 0)]
                    A("vector", lambda e, qc=qc, off=off: e.tensor_tensor(
                        RS[0:64, 2048 + 256 * (qc % 2):2048 + 256 * (qc % 2) + 256]
                        if False else RS[0:64, 2048:2560],
                        ap_nm[0:64, off:off + 512],
                        ZR[0:64, 512 * qc:512 * (qc + 1)], ALU.mult),
                      reads=KNMq[qc] + kzc, writes=krs)
                    A("tensor", lambda e, ps2=ps2: e.matmul(
                        ps2[:, 0:512], CB[0:64, B_ISHIFT:B_ISHIFT + 128],
                        RS[0:64, 2048:2560], start=True, stop=True),
                      reads=krs + KCB, writes=kps2)
                    A("vector", lambda e, p=p, qc=qc, ps2=ps2: e.tensor_copy(
                        AT[64:128, 2048 * p + 512 * qc:2048 * p + 512 * (qc + 1)],
                        ps2[64:128, 0:512]),
                      reads=kps2,
                      writes=seg_keys("at", 2048 * p + 512 * qc,
                                      2048 * p + 512 * (qc + 1)))

        def emit_qproj(p):
            slot = 2048 * (p % 2)
            for qc in range(4):
                ps, kps = SB2[qc % 2], KSC2[qc % 2]
                for ci in range(NC):
                    A("tensor", lambda e, p=p, qc=qc, ci=ci, ps=ps: e.matmul(
                        ps[:, 0:512],
                        WQ[:, 1024 * ci + 128 * p:1024 * ci + 128 * (p + 1)],
                        XB[:, 2048 * ci + 512 * qc:2048 * ci + 512 * (qc + 1)],
                        start=(ci == 0), stop=(ci == NC - 1)),
                      reads=seg_keys("wq", 1024 * ci, 1024 * (ci + 1)) +
                            seg_keys("xb", 2048 * ci + 512 * qc,
                                     2048 * ci + 512 * (qc + 1)),
                      writes=kps)
                rope_evac(kps, ps[:, 0:512], CF[:, C_BQ + p:C_BQ + p + 1], 0.125,
                          QP[:, slot + 512 * qc:slot + 512 * (qc + 1)],
                          seg_keys("qp", slot + 512 * qc, slot + 512 * (qc + 1)),
                          B_RCOS + 512 * qc, B_RSIN + 512 * qc)

        emit_qproj(0)
        emit_qproj(1)
        for h in range(16):
            emit_strips(h, 0, 7)
            if h > 0:
                emit_pv_half(h - 1, 0)
                emit_lnexp(h - 1, 0)
                emit_pv_half(h - 1, 1)
                emit_lnexp(h - 1, 1)
            emit_gc(h)
            emit_strips(h, 8, 15)
            if h > 0:
                emit_evac_rep(h - 1, 0)
                emit_evac_rep(h - 1, 1)
                emit_evac_div(h - 1, 0)
                emit_evac_div(h - 1, 1)
            emit_masks(h)
            if h % 2 == 1 and h < 12:
                emit_qproj(h // 2 + 2)
            emit_extra_pv(h)
            if h == 13:
                load_weight(lambda ci: wo_d[128 * ci:128 * (ci + 1), :], 8, WQ,
                            "wq", 0, 1024)
        emit_pv_half(15, 0)
        emit_lnexp(15, 0)
        emit_pv_half(15, 1)
        emit_lnexp(15, 1)
        emit_evac_rep(15, 0)
        emit_evac_rep(15, 1)
        emit_evac_div(15, 0)
        emit_evac_div(15, 1)
        if debug:
            ese_all = ([("ese", "es", ES_SW * tj) for tj in range(16)] +
                       [("ese", "wk", ES_SW * tj) for tj in range(6)] +
                       [("ese", "wv", ES_SW * tj) for tj in range(2)])
            for nm, t, w in [("dbg_kb", KB, 8192), ("dbg_qp", QP, 4096),
                             ("dbg_es", ES, 10304), ("dbg_vb", VB, 4160),
                             ("dbg_g4b", G4B, GB_W), ("dbg_em", EM, 68),
                             ("dbg_eqt", EQT, 64), ("dbg_xgf", XGF, 32),
                             ("dbg_kgf", KGF, 16)]:
                extra = ese_all if nm == "dbg_es" else []
                dma(dbg[nm][:], t[:, 0:w], seg_keys(nm[4:], 0, w) + extra, [])

        def sq_region(co):
            if co < 5:
                return ES, 2048 * co, "es"
            if co < 7:
                return WK, 2048 * (co - 5), "wk"
            return WV, 0, "wv"

        for co in range(NC):
            sl = (co % 2) * 2048
            ksc = seg_keys("sc", sl, sl + 2048)
            dma(SC[:, sl:sl + 2048], xfm[128 * co:128 * (co + 1), :], [], ksc)
            for qc in range(4):
                ps, kps = SB2[(co * 4 + qc) % 2], KSC2[(co * 4 + qc) % 2]
                for ci in range(NC):
                    A("tensor", lambda e, co=co, qc=qc, ci=ci, ps=ps: e.matmul(
                        ps[:, 0:512],
                        WQ[:, 1024 * ci + 128 * co:1024 * ci + 128 * (co + 1)],
                        AT[:, 2048 * ci + 512 * qc:2048 * ci + 512 * (qc + 1)],
                        start=(ci == 0), stop=(ci == NC - 1)),
                      reads=seg_keys("wq", 1024 * ci, 1024 * (ci + 1)) +
                            seg_keys("at", 2048 * ci + 512 * qc,
                                     2048 * ci + 512 * (qc + 1)),
                      writes=kps)
                A("vector", lambda e, co=co, qc=qc, sl=sl, ps=ps: e.scalar_tensor_tensor(
                    XB[:, 2048 * co + 512 * qc:2048 * co + 512 * (qc + 1)],
                    ps[:, 0:512], CF[:, C_BO + co:C_BO + co + 1],
                    SC[:, sl + 512 * qc:sl + 512 * (qc + 1)], ALU.add, ALU.add),
                  reads=kps + KCF + ksc,
                  writes=seg_keys("xb", 2048 * co + 512 * qc, 2048 * co + 512 * (qc + 1)))
            sqt, sqb, sqn = sq_region(co)
            ksq = seg_keys(sqn, sqb, sqb + 2048) + [
                ("ese", sqn, b) for b in range(0, 10304, ES_SW)
                if b + ES_SW > sqb and b < sqb + 2048 and
                (sqn != "wk" or b < ES_SW * 6) and (sqn != "wv" or b < ES_SW * 2)
                and (sqn != "es" or b < ES_SW * 16)]
            kxbc = seg_keys("xb", 2048 * co, 2048 * (co + 1))
            if co % 4 == 3:
                A("scalar", lambda e, co=co, sqt=sqt, sqb=sqb: e.activation(
                    sqt[:, sqb:sqb + 2048], XB[:, 2048 * co:2048 * (co + 1)],
                    AF.Square),
                  reads=kxbc, writes=ksq)
            else:
                A("gpsimd" if co % 4 == 1 else "vector",
                  lambda e, co=co, sqt=sqt, sqb=sqb: e.tensor_tensor(
                    sqt[:, sqb:sqb + 2048], XB[:, 2048 * co:2048 * (co + 1)],
                    XB[:, 2048 * co:2048 * (co + 1)], ALU.mult),
                  reads=kxbc, writes=ksq)

    if debug:
        dma(dbg["dbg_xb"][:], XB[:], seg_keys("xb", 0, 16384), [])

    with nc.psum_tensor("lp_sum", [1, 2048], F32) as lp_sum, \
         nc.psum_tensor("lp_sq", [1, 2048], F32) as lp_sq:
        KSUM, KSQ = psum_keys(nc, lp_sum), psum_keys(nc, lp_sq)
        for c in range(NC):
            sqt, sqb, sqn = sq_region(c)
            krs = seg_keys(sqn, sqb, sqb + 2048)
            for qc in range(4):
                A("tensor", lambda e, c=c, qc=qc: e.matmul(
                    lp_sum[0:1, 512 * qc:512 * (qc + 1)], CB[:, B_ONES:B_ONES + 1],
                    XB[:, 2048 * c + 512 * qc:2048 * c + 512 * (qc + 1)],
                    start=(c == 0), stop=(c == NC - 1)),
                  reads=seg_keys("xb", 2048 * c + 512 * qc, 2048 * c + 512 * (qc + 1)) + KCB,
                  writes=KSUM)
                A("tensor", lambda e, c=c, qc=qc, sqt=sqt, sqb=sqb: e.matmul(
                    lp_sq[0:1, 512 * qc:512 * (qc + 1)], CB[:, B_ONES:B_ONES + 1],
                    sqt[:, sqb + 512 * qc:sqb + 512 * (qc + 1)],
                    start=(c == 0), stop=(c == NC - 1)),
                  reads=krs + KCB, writes=KSQ)
        ksc2 = seg_keys("sc", 0, 4096)
        A("vector", lambda e: e.tensor_scalar(
            SC[0:1, 0:2048], lp_sum[0:1, :], 1.0 / D, None, ALU.mult),
          reads=KSUM, writes=ksc2)
        A("vector", lambda e: e.tensor_tensor(
            SC[0:1, 2048:4096], SC[0:1, 0:2048], SC[0:1, 0:2048], ALU.mult),
          reads=ksc2, writes=ksc2)
        A("vector", lambda e: e.scalar_tensor_tensor(
            SC[0:1, 2048:4096], lp_sq[0:1, :], 1.0 / D, SC[0:1, 2048:4096],
            ALU.mult, ALU.subtract),
          reads=KSQ + ksc2, writes=ksc2)
        A("scalar", lambda e: e.activation(
            SC[0:1, 2048:4096], SC[0:1, 2048:4096], AF.Ln,
            bias=CF[0:1, C_EPS:C_EPS + 1]),
          reads=ksc2 + KCF, writes=ksc2)
        A("scalar", lambda e: e.activation(
            SC[0:1, 2048:4096], SC[0:1, 2048:4096], AF.Exp, scale=-0.5),
          reads=ksc2, writes=ksc2)

    def ffn_ap(nt):
        if nt < FFN_NT_ES:
            return ES[:, 1024 * nt:1024 * (nt + 1)], seg_keys("es", 1024 * nt,
                                                              1024 * (nt + 1))
        m = nt - FFN_NT_ES
        return WQ[:, 1024 * m:1024 * (m + 1)], seg_keys("wq", 1024 * m, 1024 * (m + 1))

    def wping_ap(e_idx, ci):
        if e_idx % 2 == 0:
            return KB[:, 1024 * ci:1024 * (ci + 1)], seg_keys("kb", 1024 * ci,
                                                              1024 * (ci + 1))
        if ci < 4:
            return QP[:, 1024 * ci:1024 * (ci + 1)], seg_keys("qp", 1024 * ci,
                                                              1024 * (ci + 1))
        return VB[:, 1024 * (ci - 4):1024 * (ci - 3)], seg_keys("vb", 1024 * (ci - 4),
                                                                1024 * (ci - 3))

    with nc.psum_tensor("mp_a", [128, 1024], F32) as mp_a, \
         nc.psum_tensor("mp_b", [128, 1024], F32) as mp_b, \
         nc.psum_tensor("mp_g", [128, 32], F32) as mp_g, \
         nc.psum_tensor("mp_g2", [128, 32], F32) as mp_g2, \
         nc.psum_tensor("mp_t", [128, 128], BF16) as mp_t, \
         nc.psum_tensor("mp_t2", [128, 128], BF16) as mp_t2:
        KYA, KYB = psum_keys(nc, mp_a), psum_keys(nc, mp_b)
        KGT, KYT = psum_keys(nc, mp_g), psum_keys(nc, mp_t)
        KGT2 = psum_keys(nc, mp_g2)
        KYT2 = psum_keys(nc, mp_t2)

        for qc in range(4):
            A("tensor", lambda e, qc=qc: e.matmul(
                mp_a[:, 0:512], CF[0:1, C_ONESF:C_ONESF + 128],
                SC[0:1, 512 * qc:512 * (qc + 1)], start=True, stop=True),
              reads=ksc2 + KCF, writes=KYA)
            A("tensor", lambda e, qc=qc: e.matmul(
                mp_a[:, 512:1024], CF[0:1, C_ONESF:C_ONESF + 128],
                SC[0:1, 2048 + 512 * qc:2048 + 512 * (qc + 1)], start=True, stop=True),
              reads=ksc2 + KCF, writes=KYA)
            st_off = 0
            xv = XB[:, 512 * qc:512 * qc + 1]
            xv3 = bass.AP(xv.tensor, xv.offset, [xv.ap[0], [2048, 8], [1, 512]])
            mr = mp_a[:, 0:512]
            mr3 = bass.AP(mr.tensor, mr.offset, [mr.ap[0], [0, 8], [1, 512]])
            rr = mp_a[:, 512:1024]
            rr3 = bass.AP(rr.tensor, rr.offset, [rr.ap[0], [0, 8], [1, 512]])
            av = AT[:, 512 * qc:512 * qc + 1]
            av3 = bass.AP(av.tensor, av.offset, [av.ap[0], [2048, 8], [1, 512]])
            qp0 = QP[:, 0:1]
            qp3 = bass.AP(qp0.tensor, qp0.offset, [qp0.ap[0], [512, 8], [1, 512]])
            kall = seg_keys("xb", 0, 16384)
            kat = seg_keys("at", 0, 16384)
            kqp = seg_keys("qp", 0, 4096)
            A("vector", lambda e, xv3=xv3, mr3=mr3, qp3=qp3: e.tensor_tensor(
                qp3, xv3, mr3, ALU.subtract),
              reads=kall + KYA, writes=kqp)
            A("vector", lambda e, av3=av3, rr3=rr3, qp3=qp3: e.tensor_tensor(
                av3, qp3, rr3, ALU.mult),
              reads=kqp + KYA, writes=kat)
            gp, kgp = (mp_g, KGT) if qc % 2 == 0 else (mp_g2, KGT2)
            for i in range(4):
                nt = 4 * qc + i
                for ci in range(NC):
                    A("tensor", lambda e, nt=nt, ci=ci, i=i, gp=gp: e.matmul(
                        gp[:, 8 * i:8 * (i + 1)],
                        AT[:, 2048 * ci + 128 * nt:2048 * ci + 128 * (nt + 1)],
                        GWS[:, 8 * ci:8 * (ci + 1)], start=(ci == 0), stop=(ci == NC - 1)),
                      reads=seg_keys("at", 2048 * ci + 128 * nt,
                                     2048 * ci + 128 * (nt + 1)) + [("gw", 0)],
                      writes=kgp)

            def sv(t, off, on, os_, inn, ins=1):
                t0 = t[:, 0:1]
                return bass.AP(t0.tensor, t0.offset + off,
                               [t0.ap[0], [os_, on], [ins, inn]])
            va, vb = sv(MS, 272, 4, 8, 4), sv(MS, 276, 4, 8, 4)
            ta, tb = sv(MS, 136, 4, 4, 4), sv(MS, 152, 4, 2, 2)
            ta0, ta1 = sv(MS, 136, 4, 4, 2), sv(MS, 138, 4, 4, 2)
            tb0, tb1 = sv(MS, 152, 4, 2, 1), sv(MS, 153, 4, 2, 1)
            m1c, m2c = sv(MS, 128, 4, 1, 1), sv(MS, 132, 4, 1, 1)
            m1b, m2b = sv(MS, 128, 4, 1, 8, 0), sv(MS, 132, 4, 1, 8, 0)
            v8 = sv(MS, 272, 4, 8, 8)
            oh1 = sv(MS, 160, 4, 8, 8)
            l2v = sv(MS, 192, 4, 8, 8)
            oh2 = sv(MS, 224, 4, 8, 8)
            w1b, w2b = sv(MS, 264, 4, 1, 8, 0), sv(MS, 268, 4, 1, 8, 0)
            cfo = sv(MS, 32 * qc, 4, 8, 8)
            A("vector", lambda e, gp=gp: e.tensor_copy(
                MS[:, 272:304], gp[:, 0:32]), reads=kgp, writes=KMS)
            A("vector", lambda e, va=va, vb=vb, ta=ta: e.tensor_tensor(
                ta, va, vb, ALU.max), reads=KMS, writes=KMS)
            A("vector", lambda e, ta0=ta0, ta1=ta1, tb=tb: e.tensor_tensor(
                tb, ta0, ta1, ALU.max), reads=KMS, writes=KMS)
            A("vector", lambda e, tb0=tb0, tb1=tb1, m1c=m1c: e.tensor_tensor(
                m1c, tb0, tb1, ALU.max), reads=KMS, writes=KMS)
            A("vector", lambda e, v8=v8, m1b=m1b, oh1=oh1: e.tensor_tensor(
                oh1, v8, m1b, ALU.is_equal), reads=KMS, writes=KMS)
            A("vector", lambda e, v8=v8, oh1=oh1, l2v=l2v: e.scalar_tensor_tensor(
                l2v, oh1, -1e9, v8, ALU.mult, ALU.add),
              reads=KMS, writes=KMS)
            la, lb = sv(MS, 192, 4, 8, 4), sv(MS, 196, 4, 8, 4)
            A("vector", lambda e, la=la, lb=lb, ta=ta: e.tensor_tensor(
                ta, la, lb, ALU.max), reads=KMS, writes=KMS)
            A("vector", lambda e, ta0=ta0, ta1=ta1, tb=tb: e.tensor_tensor(
                tb, ta0, ta1, ALU.max), reads=KMS, writes=KMS)
            A("vector", lambda e, tb0=tb0, tb1=tb1, m2c=m2c: e.tensor_tensor(
                m2c, tb0, tb1, ALU.max), reads=KMS, writes=KMS)
            A("vector", lambda e: e.tensor_tensor(
                MS[:, 256:260], MS[:, 132:136], MS[:, 128:132], ALU.subtract),
              reads=KMS, writes=KMS)
            A("scalar", lambda e: e.activation(
                MS[:, 260:264], MS[:, 256:260], AF.Exp),
              reads=KMS, writes=KMS)
            A("vector", lambda e: e.tensor_scalar(
                MS[:, 260:264], MS[:, 260:264], 1.0, None, ALU.add),
              reads=KMS, writes=KMS)
            A("vector", lambda e: e.reciprocal(MS[:, 264:268], MS[:, 260:264]),
              reads=KMS, writes=KMS)
            A("vector", lambda e: e.tensor_scalar(
                MS[:, 268:272], MS[:, 264:268], -1.0, 1.0, ALU.mult, ALU.add),
              reads=KMS, writes=KMS)
            A("vector", lambda e, l2v=l2v, m2b=m2b, oh2=oh2: e.tensor_tensor(
                oh2, l2v, m2b, ALU.is_equal), reads=KMS, writes=KMS)
            A("vector", lambda e, oh1=oh1, w1b=w1b, cfo=cfo: e.tensor_tensor(
                cfo, oh1, w1b, ALU.mult), reads=KMS, writes=KMS)
            A("vector", lambda e, oh2=oh2, w2b=w2b: e.tensor_tensor(
                oh2, oh2, w2b, ALU.mult), reads=KMS, writes=KMS)
            A("vector", lambda e, qc=qc, oh2=oh2, cfo=cfo: e.tensor_tensor(
                cfo, cfo, oh2, ALU.add), reads=KMS, writes=KMS)

        if debug:
            dma(dbg["dbg_ms"][:], MS[:], KMS, [])

        for ei in range(8):
            for ci in range(NC):
                sl = (wslot[0] % 4) * 1024
                wslot[0] += 1
                ksc = seg_keys("sc", sl, sl + 1024)
                dma(SC[:, sl:sl + 1024], ew_d[ei, 128 * ci:128 * (ci + 1), :], [], ksc)
                wap, kw = wping_ap(ei, ci)
                A("gpsimd", lambda e, sl=sl, wap=wap: e.tensor_copy(wap, SC[:, sl:sl + 1024]),
                  reads=ksc, writes=kw)
            for nt in range(NT):
                ye, kye = (mp_a, KYA) if (ei * NT + nt) % 2 == 0 else (mp_b, KYB)
                for ci in range(NC):
                    wap, kw = wping_ap(ei, ci)
                    for half in range(2):
                        A("tensor", lambda e, nt=nt, ci=ci, half=half, ye=ye, wap=wap: e.matmul(
                            ye[:, 512 * half:512 * (half + 1)],
                            AT[:, 2048 * ci + 128 * nt:2048 * ci + 128 * (nt + 1)],
                            wap[:, 512 * half:512 * (half + 1)],
                            start=(ci == 0), stop=(ci == NC - 1)),
                          reads=seg_keys("at", 2048 * ci + 128 * nt,
                                         2048 * ci + 128 * (nt + 1)) + kw,
                          writes=kye)
                fap, kf = ffn_ap(nt)
                if ei == 0:
                    A("vector", lambda e, nt=nt, ei=ei, ye=ye, fap=fap: e.tensor_scalar(
                        fap, ye[:], MS[:, 8 * nt + ei:8 * nt + ei + 1], None, ALU.mult),
                      reads=kye + KMS, writes=kf)
                else:
                    A("vector", lambda e, nt=nt, ei=ei, ye=ye, fap=fap: e.scalar_tensor_tensor(
                        fap, ye[:], MS[:, 8 * nt + ei:8 * nt + ei + 1], fap,
                        ALU.mult, ALU.add),
                      reads=kye + KMS + kf, writes=kf)
                if ei == 7:
                    par = nt % 2
                    par4 = nt % 4
                    mt, kyt = (mp_t, KYT) if par == 0 else (mp_t2, KYT2)
                    krs = [("rsy", par)]
                    kms2 = [("ms2", nt)]
                    mb = 192 + 8 * nt
                    for c in range(NC):
                        A("tensor", lambda e, c=c, nt=nt, mt=mt: e.transpose(
                            mt[:], AT[:, 2048 * c + 128 * nt:2048 * c + 128 * (nt + 1)],
                            CB[:, B_IDEN:B_IDEN + 128]),
                          reads=seg_keys("at", 2048 * c + 128 * nt,
                                         2048 * c + 128 * (nt + 1)) + KCB,
                          writes=kyt)
                        A("scalar", lambda e, c=c, par=par, mt=mt: e.copy(
                            RS[:, 1024 * par + 128 * c:1024 * par + 128 * (c + 1)], mt[:]),
                          reads=kyt, writes=krs)
                    ksc3 = [("scr", par4)]
                    A("vector", lambda e, fap=fap, par=par, par4=par4, mb=mb:
                      e.scalar_tensor_tensor(
                        KB[:, 1024 * par4:1024 * (par4 + 1)],
                        RS[:, 1024 * par:1024 * (par + 1)], 1.0, fap,
                        ALU.mult, ALU.add, accum_out=MS[:, mb:mb + 1]),
                      reads=krs + kf, writes=ksc3 + kms2)
                    A("vector", lambda e, par=par, par4=par4, mb=mb:
                      e.scalar_tensor_tensor(
                        KB[:, 4096 + 1024 * par:4096 + 1024 * (par + 1)],
                        KB[:, 1024 * par4:1024 * (par4 + 1)], 1.0,
                        KB[:, 1024 * par4:1024 * (par4 + 1)],
                        ALU.mult, ALU.mult, accum_out=MS[:, mb + 1:mb + 2]),
                      reads=ksc3, writes=[("kbs", par)] + kms2)
                    A("vector", lambda e, mb=mb: e.tensor_scalar(
                        MS[:, mb + 2:mb + 3], MS[:, mb:mb + 1], 1.0 / D, None, ALU.mult),
                      reads=kms2, writes=kms2)
                    A("vector", lambda e, mb=mb: e.tensor_tensor(
                        MS[:, mb + 3:mb + 4], MS[:, mb + 2:mb + 3], MS[:, mb + 2:mb + 3],
                        ALU.mult),
                      reads=kms2, writes=kms2)
                    A("vector", lambda e, mb=mb: e.scalar_tensor_tensor(
                        MS[:, mb + 4:mb + 5], MS[:, mb + 1:mb + 2], 1.0 / D,
                        MS[:, mb + 3:mb + 4], ALU.mult, ALU.subtract),
                      reads=kms2, writes=kms2)
                    A("scalar", lambda e, mb=mb: e.activation(
                        MS[:, mb + 5:mb + 6], MS[:, mb + 4:mb + 5], AF.Ln,
                        bias=CF[:, C_EPS:C_EPS + 1]),
                      reads=kms2 + KCF, writes=kms2)
                    A("scalar", lambda e, mb=mb: e.activation(
                        MS[:, mb + 6:mb + 7], MS[:, mb + 5:mb + 6], AF.Exp, scale=-0.5),
                      reads=kms2, writes=kms2)
                    so = 1024 * par4
                    kso = [("sco", par4)]
                    A("vector", lambda e, so=so, mb=mb, par4=par4: e.tensor_scalar(
                        SC[:, so:so + 1024], KB[:, 1024 * par4:1024 * (par4 + 1)],
                        MS[:, mb + 2:mb + 3], MS[:, mb + 6:mb + 7],
                        ALU.subtract, ALU.mult),
                      reads=ksc3 + kms2, writes=kso)
                    dma(out_d[128 * nt:128 * (nt + 1), :], SC[:, so:so + 1024], kso, [])

    if debug:
        dma(dbg["dbg_at"][:], AT[:], seg_keys("at", 0, 16384), [])

    with nc.Block() as block:
        sch.emit(block)
    stack.close()
    return nc


def prepare_inputs(inputs):
    x = np.asarray(inputs["L0_attn_x"], dtype=np.float32)
    gpos = np.asarray(inputs["L0_attn_global_positions"]).astype(np.int64)
    Wq = np.asarray(inputs["L0_attn_Wq"], dtype=np.float32)
    bq = np.asarray(inputs["L0_attn_bq"], dtype=np.float32)
    Wk = np.asarray(inputs["L0_attn_Wk"], dtype=np.float32)
    bk = np.asarray(inputs["L0_attn_bk"], dtype=np.float32)
    Wv = np.asarray(inputs["L0_attn_Wv"], dtype=np.float32)
    bv = np.asarray(inputs["L0_attn_bv"], dtype=np.float32)
    Wo = np.asarray(inputs["L0_attn_Wo"], dtype=np.float32)
    bo = np.asarray(inputs["L0_attn_bo"], dtype=np.float32)
    gw = np.asarray(inputs["L0_ffn_gate_w"], dtype=np.float32)
    ew = np.asarray(inputs["L0_ffn_expert_w"], dtype=np.float32)

    Wq_p = np.ascontiguousarray(Wq.reshape(D, H, HD)[:, :, PERM64].reshape(D, D))
    bq_p = bq.reshape(H, HD)[:, PERM64].reshape(D)
    Wk_p = Wk.reshape(D, G, HD)[:, :, PERM64]
    Wk_pd = np.ascontiguousarray(
        np.concatenate([Wk_p, Wk_p], axis=2).reshape(D, 512))
    bk_p = bk.reshape(G, HD)[:, PERM64]
    bk_pd = np.concatenate([bk_p, bk_p], axis=1).reshape(512)
    gwf = np.ascontiguousarray(gw.T.reshape(8, 128, 8).transpose(1, 0, 2)
                               .reshape(128, 64))
    ewt = np.ascontiguousarray(ew.transpose(0, 2, 1))

    inv_freq = 1.0 / 10000.0 ** (np.arange(0, HD, 2, dtype=np.float64) / HD)
    t = np.arange(L, dtype=np.float64)
    emb = np.concatenate([t[:, None] * inv_freq[None, :]] * 2, axis=1)
    cos_pl = np.cos(emb[:, PERM64])
    sin_pl = np.sin(emb[:, PERM64]) * SGN64[None, :]
    rcos = np.tile(cos_pl.T, (2, 1)).astype(BF)
    rsin = np.tile(sin_pl.T, (2, 1)).astype(BF)

    cf = np.zeros((128, CF_W), dtype=np.float32)
    cf[:, C_IOTAC:C_IOTAC + NT] = (np.arange(NT)[None, :] * 128
                                   + np.arange(128)[:, None]).astype(np.float32)
    cf[:, C_BQ:C_BQ + 8] = bq_p.reshape(8, 128).T
    cf[:, C_BK:C_BK + 4] = bk_pd.reshape(4, 128).T
    cf[:, C_BO:C_BO + 8] = bo.reshape(8, 128).T
    cf[:, C_ONESF:C_ONESF + 128] = 1.0
    cf[:, C_EPS] = 1e-5
    r4 = np.arange(4)
    cf[0:4, C_LC:C_LC + 4] = (r4[None, :] < r4[:, None]).astype(np.float32)
    cf[0:4, C_LCT:C_LCT + 4] = (r4[:, None] < r4[None, :]).astype(np.float32)

    cb = np.zeros((128, CB_W), dtype=BF)
    a128 = np.arange(128)
    cb[:, B_IDEN:B_IDEN + 128] = np.eye(128, dtype=BF)
    cb[:, B_ONES:B_ONES + 128] = np.float32(1.0)
    cb[:, B_MASKL:B_MASKL + 128] = (a128[None, :] >= a128[:, None]).astype(BF)
    cb[:, B_MASKR:B_MASKR + 128] = (a128[None, :] <= a128[:, None]).astype(BF)
    ish = np.zeros((128, 128), dtype=np.float32)
    ish[np.arange(64), np.arange(64) + 64] = 1.0
    cb[:, B_ISHIFT:B_ISHIFT + 128] = ish.astype(BF)
    cb[:, B_RCOS:B_RCOS + L] = rcos
    cb[:, B_RSIN:B_RSIN + L] = rsin

    tt = np.zeros((128, 2048), dtype=BF)
    for tt_i in range(NT):
        rows = slice(128 * tt_i, 128 * (tt_i + 1))
        tt[:, 128 * tt_i:128 * tt_i + 64] = cos_pl[rows].astype(BF)
        tt[:, 128 * tt_i + 64:128 * tt_i + 128] = sin_pl[rows].astype(BF)

    brows = np.zeros((1, 2048), dtype=np.float32)
    brows[0, 0:1024] = bq_p
    brows[0, 1024:1536] = bk_pd
    brows[0, 1536:1792] = bv

    gpr = np.tile(gpos.astype(np.float32)[None, :], (128, 1))
    gpc = gpos.astype(np.float32).reshape(4, 1)
    gpi = gpos.astype(np.int32).reshape(4, 1)

    shared = {
        "wq": Wq_p, "wk": Wk_pd, "wv": np.ascontiguousarray(Wv),
        "wo": np.ascontiguousarray(Wo), "gwf": gwf, "ewt": ewt,
        "cf": cf, "cb": cb, "ttok": tt, "brows": brows,
        "gpr": gpr, "gpc": gpc, "gpi": gpi,
    }
    in_maps = []
    for b in range(8):
        m = dict(shared)
        m["xfm"] = np.ascontiguousarray(x[b].T)
        m["xstd"] = np.ascontiguousarray(x[b])
        in_maps.append(m)
    return in_maps


def kernel(**inputs):
    in_maps = prepare_inputs(inputs)
    nc = build_program()
    res = run_bass_kernel_spmd(nc, in_maps, core_ids=list(range(8)))
    out = np.stack([np.asarray(r["out"], dtype=np.float32) for r in res.results])
    return out
